# version 11
# baseline (speedup 1.0000x reference)
import numpy as np
from contextlib import ExitStack

import concourse.bass as bass
import concourse.mybir as mybir
from concourse.bass_utils import run_bass_kernel_spmd

F32 = mybir.dt.float32
BF16 = mybir.dt.bfloat16
I8 = mybir.dt.int8
ALU = mybir.AluOpType
AF = mybir.ActivationFunctionType
AX = mybir.AxisListType

NF = 4096
G = 32
T = 1024
S = 4096
H = 32
KVH = 8
SLW = 256
NSL_Q = NF // SLW
KGS = 512
NKG = S // KGS
NEG = -1e30
EPS = 1e-5

ENGINES = ("sync", "gpsimd", "vector", "scalar", "tensor")
SEMS = ("LD", "LD2", "PE", "VE", "AC", "GP", "ST", "CC")


class Plan:
    def __init__(self, nc, stack):
        self.nc = nc
        self.q = {e: [] for e in ENGINES}
        self.trace = {e: [] for e in ENGINES}
        self.maxwait = {}
        self.sems = {}
        self.cnt = {}
        for name in SEMS:
            self.sems[name] = stack.enter_context(nc.semaphore(name))
            self.cnt[name] = 0

    def emit(self, eng, f, inc=None, n=1):
        if inc is not None:
            self.cnt[inc] += n
            sem = self.sems[inc]
            self.q[eng].append(lambda E, f=f, sem=sem, n=n: f(E).then_inc(sem, n))
            self.trace[eng].append(("inc", inc, n))
            return self.cnt[inc]
        self.q[eng].append(f)
        self.trace[eng].append(("op",))
        return None

    def wait(self, eng, semname, count):
        if count is None or count <= 0:
            return
        key = (eng, semname)
        if self.maxwait.get(key, -1) >= count:
            return
        self.maxwait[key] = count
        sem = self.sems[semname]
        self.q[eng].append(lambda E, sem=sem, c=count: E.wait_ge(sem, c))
        self.trace[eng].append(("wait", semname, count))

    def c(self, semname):
        return self.cnt[semname]

    def barrier(self):
        snap = dict(self.cnt)
        for e in ENGINES:
            for s in SEMS:
                if s == "CC":
                    continue
                self.wait(e, s, snap[s])


class V:
    def __init__(self, h, c0, cols, p0=0, parts=128):
        self.h, self.c0, self.cols, self.p0, self.parts = h, c0, cols, p0, parts

    def _pc(self, idx):
        if idx == slice(None):
            return (self.p0, self.p0 + self.parts, self.c0, self.c0 + self.cols)
        p, c = idx
        ps = self.p0 + (p.start or 0)
        pe = self.p0 + (self.parts if p.stop is None else p.stop)
        cs = self.c0 + (c.start or 0)
        ce = self.c0 + (self.cols if c.stop is None else c.stop)
        return (ps, pe, cs, ce)

    def __getitem__(self, idx):
        ps, pe, cs, ce = self._pc(idx)
        return self.h[ps:pe, cs:ce]


def build_program(debug=False):
    nc = bass.Bass("TRN2", target_bir_lowering=False)

    ap = lambda t: t.ap()
    xq_t = ap(nc.dram_tensor("xq_t", [NF, T], F32, kind="ExternalInput"))
    qcod = ap(nc.dram_tensor("qcod", [NSL_Q, 128, G, SLW], I8, kind="ExternalInput"))
    ocod = ap(nc.dram_tensor("ocod", [NSL_Q, 128, G, SLW], I8, kind="ExternalInput"))
    kcod = ap(nc.dram_tensor("kcod", [4, 128, G, SLW], I8, kind="ExternalInput"))
    vcod = ap(nc.dram_tensor("vcod", [4, 128, G, SLW], I8, kind="ExternalInput"))
    qsc = ap(nc.dram_tensor("qsc", [G, NF], F32, kind="ExternalInput"))
    osc = ap(nc.dram_tensor("osc", [G, NF], F32, kind="ExternalInput"))
    ksc = ap(nc.dram_tensor("ksc", [G, 1024], F32, kind="ExternalInput"))
    vsc = ap(nc.dram_tensor("vsc", [G, 1024], F32, kind="ExternalInput"))
    aw_t = ap(nc.dram_tensor("aw_t", [G, 128, 512], F32, kind="ExternalInput"))
    bw_t = ap(nc.dram_tensor("bw_t", [8, 4, 128, 512], F32, kind="ExternalInput"))
    wqw_t = ap(nc.dram_tensor("wqw_t", [G, 128, 16], F32, kind="ExternalInput"))
    keys_t = ap(nc.dram_tensor("keys_t", [16, 64], F32, kind="ExternalInput"))
    ropeq_cc = ap(nc.dram_tensor("ropeq_cc", [128, T], F32, kind="ExternalInput"))
    ropeq_ss = ap(nc.dram_tensor("ropeq_ss", [128, T], F32, kind="ExternalInput"))
    ropeko_cc = ap(nc.dram_tensor("ropeko_cc", [128, T], F32, kind="ExternalInput"))
    ropeko_ss = ap(nc.dram_tensor("ropeko_ss", [128, T], F32, kind="ExternalInput"))
    kqpos = ap(nc.dram_tensor("kqpos", [1, T], F32, kind="ExternalInput"))
    kpcol = ap(nc.dram_tensor("kpcol", [128, 32], F32, kind="ExternalInput"))
    biascol_i = ap(nc.dram_tensor("biascol_i", [128, 32], F32, kind="ExternalInput"))
    jmat_i = ap(nc.dram_tensor("jmat_i", [128, 128], F32, kind="ExternalInput"))
    ident_i = ap(nc.dram_tensor("ident_i", [128, 128], F32, kind="ExternalInput"))
    rrep_i = ap(nc.dram_tensor("rrep_i", [4, 64, 128], F32, kind="ExternalInput"))
    consts_i = ap(nc.dram_tensor("consts_i", [128, 8], F32, kind="ExternalInput"))

    out_t = ap(nc.dram_tensor("out_t", [G, 128, T], F32, kind="ExternalOutput"))

    kind_s = "ExternalOutput" if debug else "Internal"
    wkv_scr = ap(nc.dram_tensor("wkv_scr", [8, G, 128, SLW], BF16, kind="Internal"))
    q_scr = ap(nc.dram_tensor("q_scr", [H, 128, T], BF16, kind=kind_s))
    k_own_t = nc.dram_tensor("k_own", [KVH * 8, 128 * 128], BF16, kind="Internal")
    v_own_t = nc.dram_tensor("v_own", [KVH * 8, 128 * 128], BF16, kind="Internal")
    k_gath_t = nc.dram_tensor("k_gath", [4 * KVH * 8, 128 * 128], BF16, kind="Internal")
    v_gath_t = nc.dram_tensor("v_gath", [4 * KVH * 8, 128 * 128], BF16, kind="Internal")
    mk_own_t = nc.dram_tensor("mk_own", [1, 8], F32, kind="Internal")
    mk_red_t = nc.dram_tensor("mk_red", [1, 8], F32, kind="Internal")
    k_own, v_own = ap(k_own_t), ap(v_own_t)
    k_gath, v_gath = ap(k_gath_t), ap(v_gath_t)
    mk_own, mk_red = ap(mk_own_t), ap(mk_red_t)
    x1_scr = ap(nc.dram_tensor("x1_scr", [G, 128, T], F32, kind=kind_s))
    schq = ap(nc.dram_tensor("schq", [G, NF], BF16, kind="Internal"))
    scho = ap(nc.dram_tensor("scho", [G, NF], BF16, kind="Internal"))
    schk = ap(nc.dram_tensor("schk", [G, 1024], BF16, kind="Internal"))
    schv = ap(nc.dram_tensor("schv", [G, 1024], BF16, kind="Internal"))
    dbg = {}
    if debug:
        for nm, sh, dt in [("d_xn1", [128, G * T], BF16), ("d_dens", [32, T], F32),
                           ("d_negm", [128, 32], F32), ("d_gate", [64, T], F32),
                           ("d_zg", [128, 4 * T], BF16), ("d_xn2", [128, G * T], BF16)]:
            dbg[nm] = ap(nc.dram_tensor(nm, sh, dt, kind="ExternalOutput"))

    with ExitStack() as stack:
        P = Plan(nc, stack)
        sb = lambda name, shape, dt: stack.enter_context(nc.sbuf_tensor(name, shape, dt))

        consts = sb("consts", [128, 8], F32)
        onesrow = sb("onesrow", [1, 128], F32)
        onescol_bf = sb("onescol_bf", [128, 1], BF16)
        jmat = sb("jmat", [128, 128], F32)
        ident = sb("ident", [128, 128], F32)
        biascol = sb("biascol", [128, 32], F32)
        negmb = sb("negmb", [128, 32], F32)
        smallA = sb("smallA", [128, 80], F32)
        rowsA = sb("rowsA", [1, 3 * T], F32)
        rowsB = sb("rowsB", [1, 320], F32)
        RBt = sb("RBt", [128, 2 * T], F32)
        MB = sb("MB", [128, T], F32)
        RD = sb("RD", [128, T], F32)
        jnk = sb("jnk", [128, 64], F32)
        ceps = sb("ceps", [128, 1], F32)
        nc.const_aps.register(F32, EPS, ceps.ap())
        cods8 = sb("cods8", [128, G * SLW], I8)
        ACT1 = sb("ACT1", [128, G * T], BF16)
        ACT2 = sb("ACT2", [128, G * 512], BF16)
        KVW = sb("KVW", [128, G * 512], BF16)
        BFS = sb("BFS", [128, 4096], BF16)
        ACT3 = sb("ACT3", [128, 8192], F32)
        PA = stack.enter_context(nc.psum_tensor("PA", [128, T], F32))
        PB = stack.enter_context(nc.psum_tensor("PB", [128, T], F32))
        PC = stack.enter_context(nc.psum_tensor("PC", [128, T], F32))
        PD = stack.enter_context(nc.psum_tensor("PD", [128, T], F32))

        ln_m = V(rowsA, 0, T, parts=1)
        ln_s2 = V(rowsA, T, T, parts=1)
        ln_rstd = V(rowsA, 2 * T, T, parts=1)
        Rb = V(RBt, 0, T)
        nMRb = V(RBt, T, T)
        Mb2raw = V(MB, 0, T)
        dens32 = V(MB, 0, T, parts=32)
        rd32 = V(RD, 0, T, parts=32)
        denstage = [V(ACT3, 2 * T, T, parts=1), V(ACT3, 3 * T, T, parts=1)]
        rstage = denstage
        mkr = V(rowsB, 0, 1, parts=1)
        prod = V(rowsB, 32, 32, parts=1)
        negm = V(rowsB, 64, 32, parts=1)
        biasH = [V(smallA, 0, 32), V(smallA, 32, 32)]
        stg = [V(ACT3, 0, T), V(ACT3, T, T)]
        sbcs = V(KVW, 0, G * SLW)

        def vflush():
            P.emit("vector", lambda E: E.tensor_copy(jnk[:], ident[:, 0:64]))

        for apin, dst in [(consts_i, consts), (jmat_i, jmat), (ident_i, ident),
                          (biascol_i, biascol)]:
            P.emit("sync", lambda E, a=apin, d=dst: E.dma_start(out=d[:], in_=a), inc="LD", n=16)
        P.wait("vector", "LD", P.c("LD"))
        P.emit("vector", lambda E: E.memset(onesrow[:], 1.0))
        P.emit("vector", lambda E: E.memset(onescol_bf[:], 1.0))
        P.emit("vector", lambda E: E.memset(jnk[:], 0.0))
        P.emit("vector", lambda E: E.memset(ceps[:], EPS))

        mq2row = V(rowsB, 96, 32, parts=1)
        mk2row = V(rowsB, 128, 8, parts=1)
        ve_init = P.emit("vector", lambda E: E.memset(mk2row[:], -1e30), inc="VE")

        scf = V(ACT3, 2 * T, NF, parts=32)
        sch_b = V(ACT2, 0, NF, parts=32)
        for nmx, src, dst, n in [("q", qsc, schq, NF), ("o", osc, scho, NF),
                                 ("k", ksc, schk, 1024), ("v", vsc, schv, 1024)]:
            P.wait("sync", "VE", P.c("VE"))
            c0 = P.emit("sync", lambda E, s=src, n=n: E.dma_start(out=scf[0:32, 0:n], in_=s), inc="LD", n=16)
            P.wait("vector", "LD", c0)
            P.wait("vector", "ST", P.c("ST"))
            v0 = P.emit("vector", lambda E, n=n: E.tensor_scalar(sch_b[0:32, 0:n], scf[0:32, 0:n], 0.5, None, ALU.mult), inc="VE")
            P.wait("gpsimd", "VE", v0)
            P.emit("gpsimd", lambda E, d=dst, n=n: E.dma_start(out=d, in_=sch_b[0:32, 0:n]), inc="ST", n=16)
        P.barrier()

        dq_state = {"last_stt": 0}

        def dequant_slab(wslab, codes_src, s_idx, sch, col0, free_gate):
            P.wait("sync", "VE", dq_state["last_stt"])
            c1 = P.emit("sync", lambda E, a=codes_src, s=s_idx:
                        E.dma_start(out=cods8[:].rearrange("p (g n) -> p g n", n=SLW), in_=a[s]),
                        inc="LD", n=16)
            c2 = P.emit("sync", lambda E, a=sch, c0=col0:
                        E.dma_start(out=sbcs[:].rearrange("p (g n) -> p g n", n=SLW),
                                    in_=a[0:G, c0:c0 + SLW].partition_broadcast(128)),
                        inc="LD2", n=16)
            P.wait("vector", "LD", c1)
            P.wait("vector", "LD2", c2)
            if free_gate is not None:
                P.wait("vector", free_gate[0], free_gate[1])
            for g in range(G):
                P.emit("vector", lambda E, d=wslab, g=g:
                       E.scalar_tensor_tensor(d[:, g * SLW:(g + 1) * SLW], cods8[:, g * SLW:(g + 1) * SLW], 8.0,
                                              sbcs[:, g * SLW:(g + 1) * SLW], ALU.subtract, ALU.mult), inc="VE")
            dq_state["last_stt"] = P.c("VE")
            return dq_state["last_stt"]

        lnxb = [V(ACT3, 2 * T, T), V(ACT3, 3 * T, T)]
        lnsq = V(ACT3, 4 * T, T)
        LN = {"xfree": {}, "sqfree": 0}

        def ln_stats_and_bcast(xsrc_fn, ncols, mp, s2p, rbp, mrbp):
            xb, sq = lnxb, lnsq
            nh = (ncols + 511) // 512
            for g in range(G):
                i = g % 2
                P.wait("sync", "PE", LN["xfree"].get(i, 0))
                c1 = P.emit("sync", lambda E, a=xsrc_fn(g), b=xb[i], n=ncols:
                            E.dma_start(out=b[:, 0:n], in_=a), inc="LD", n=16)
                P.wait("scalar", "LD", c1)
                P.wait("scalar", "PE", LN["sqfree"])
                a1 = P.emit("scalar", lambda E, b=xb[i], n=ncols:
                            E.activation(sq[:, 0:n], b[:, 0:n], AF.Square), inc="AC")
                P.wait("tensor", "LD", c1)
                P.wait("tensor", "AC", a1)
                for hf in range(nh):
                    lo, hi = hf * 512, min((hf + 1) * 512, ncols)
                    P.emit("tensor", lambda E, b=xb[i], lo=lo, hi=hi, g=g:
                           E.matmul(mp[0:1, lo:hi], consts[:, 2:3], b[:, lo:hi],
                                    start=(g == 0), stop=(g == G - 1)))
                    P.emit("tensor", lambda E, lo=lo, hi=hi, g=g:
                           E.matmul(s2p[0:1, lo:hi], consts[:, 2:3], sq[:, lo:hi],
                                    start=(g == 0), stop=(g == G - 1)),
                           inc="PE" if (hf == nh - 1) else None)
                LN["xfree"][i] = P.c("PE")
                LN["sqfree"] = P.c("PE")
            pe_stats = P.c("PE")
            P.wait("vector", "PE", pe_stats)
            P.emit("vector", lambda E, n=ncols: E.tensor_copy(ln_m[:, 0:n], mp[0:1, 0:n]))
            P.emit("vector", lambda E, n=ncols: E.tensor_copy(ln_s2[:, 0:n], s2p[0:1, 0:n]))
            P.emit("vector", lambda E, n=ncols: E.tensor_mul(stg[0][0:1, 0:n], ln_m[:, 0:n], ln_m[:, 0:n]))
            v1 = P.emit("vector", lambda E, n=ncols: E.tensor_sub(ln_s2[:, 0:n], ln_s2[:, 0:n], stg[0][0:1, 0:n]), inc="VE")
            P.wait("scalar", "VE", v1)
            P.emit("scalar", lambda E, n=ncols: E.activation(ln_s2[:, 0:n], ln_s2[:, 0:n], AF.Ln, bias=EPS))
            a2 = P.emit("scalar", lambda E, n=ncols: E.activation(ln_rstd[:, 0:n], ln_s2[:, 0:n], AF.Exp, scale=-0.5), inc="AC")
            P.wait("vector", "AC", a2)
            v2 = P.emit("vector", lambda E, n=ncols: E.tensor_mul(ln_m[:, 0:n], ln_m[:, 0:n], ln_rstd[:, 0:n]), inc="VE")
            P.wait("tensor", "AC", a2)
            P.wait("tensor", "VE", v2)
            for hf in range(nh):
                lo, hi = hf * 512, min((hf + 1) * 512, ncols)
                P.emit("tensor", lambda E, lo=lo, hi=hi: E.matmul(rbp[:, lo:hi], onesrow[:], ln_rstd[0:1, lo:hi], start=True, stop=True))
                P.emit("tensor", lambda E, lo=lo, hi=hi: E.matmul(mrbp[:, lo:hi], onesrow[:], ln_m[0:1, lo:hi], start=True, stop=True),
                       inc="PE" if hf == nh - 1 else None)
            pe_b = P.c("PE")
            P.wait("vector", "PE", pe_b)
            P.emit("vector", lambda E, n=ncols: E.tensor_copy(Rb[:, 0:n], rbp[:, 0:n]))
            ve_done = P.emit("vector", lambda E, n=ncols: E.tensor_scalar(nMRb[:, 0:n], mrbp[:, 0:n], -1.0, None, ALU.mult), inc="VE")
            return ve_done

        xn1 = V(ACT1, 0, G * T)
        ve_ln1 = ln_stats_and_bcast(lambda g: xq_t[g * 128:(g + 1) * 128, :], T,
                                    V(PC, 0, T, parts=1), V(PD, 0, T, parts=1),
                                    V(PA, 0, T), V(PB, 0, T))
        for g in range(G):
            i = g % 2
            P.wait("sync", "VE", LN["xfree"].get(("n", i), 0))
            P.wait("sync", "PE", LN["xfree"].get(i, 0))
            c1 = P.emit("sync", lambda E, g=g, b=lnxb[i]: E.dma_start(out=b[:], in_=xq_t[g * 128:(g + 1) * 128, :]), inc="LD", n=16)
            P.wait("vector", "LD", c1)
            if g == 0:
                P.wait("vector", "VE", ve_ln1)
            P.emit("vector", lambda E, b=lnxb[i]: E.tensor_mul(stg[0][:], b[:], Rb[:]))
            LN["xfree"][("n", i)] = P.emit("vector", lambda E, g=g: E.tensor_add(xn1[:, g * T:(g + 1) * T], stg[0][:], nMRb[:]), inc="VE")
        if debug:
            P.wait("gpsimd", "VE", P.c("VE"))
            P.emit("gpsimd", lambda E: E.dma_start(out=dbg["d_xn1"], in_=xn1[:]), inc="ST", n=16)
        P.barrier()

        wslab = [V(ACT2, 0, G * SLW), V(ACT2, G * SLW, G * SLW)]

        kocc = V(ACT3, 2 * T, T)
        koss = V(ACT3, 3 * T, T)
        kraw = V(ACT3, 4 * T, T)
        ks2 = V(ACT3, 5 * T, T)
        kvw = V(ACT2, 0, G * SLW)
        kstg = [V(KVW, kvh * T, T) for kvh in range(KVH)]
        vstg = [V(KVW, 8192 + kvh * T, T) for kvh in range(KVH)]
        kvp = [V(PA, 0, T), V(PB, 0, T)]
        jkp = V(PC, 0, T)
        kn2p = V(PD, 0, T, parts=1)
        vtp = V(PD, 0, T)
        c1 = P.emit("sync", lambda E: E.dma_start(out=kocc[:], in_=ropeko_cc), inc="LD", n=16)
        c2 = P.emit("sync", lambda E: E.dma_start(out=koss[:], in_=ropeko_ss), inc="LD", n=16)
        ld_ropek = P.c("LD")
        sbcs_h = V(BFS, 0, 4096)
        slab_free_wb = {}
        kraw_pe = 0
        psum_free = {}
        jk_free = 0
        kn_ve = 0
        stg_ve = 0
        vraw_ac = {}
        for si in range(8):
            wb = si % 2
            cod = kcod if si < 4 else vcod
            sch = schk if si < 4 else schv
            sidx = si % 4
            P.wait("sync", "VE", dq_state["last_stt"])
            c1k = P.emit("sync", lambda E, a=cod, s=sidx:
                         E.dma_start(out=cods8[:].rearrange("p (g n) -> p g n", n=SLW), in_=a[s]),
                         inc="LD", n=16)
            P.wait("vector", "LD", c1k)
            P.wait("vector", "PE", slab_free_wb.get(wb, 0))
            for half in range(2):
                if half == 1:
                    P.wait("sync", "VE", P.c("VE"))
                c2k = P.emit("sync", lambda E, a=sch, c0=sidx * SLW, half=half:
                             E.dma_start(out=sbcs_h[:].rearrange("p (g n) -> p g n", n=SLW),
                                         in_=a[half * 16:(half + 1) * 16, c0:c0 + SLW].partition_broadcast(128)),
                             inc="LD2", n=16)
                P.wait("vector", "LD2", c2k)
                for g16 in range(16):
                    g = half * 16 + g16
                    P.emit("vector", lambda E, wb=wb, g=g, g16=g16:
                           E.scalar_tensor_tensor(wslab[wb][:, g * SLW:(g + 1) * SLW],
                                                  cods8[:, g * SLW:(g + 1) * SLW], 8.0,
                                                  sbcs_h[:, g16 * SLW:(g16 + 1) * SLW],
                                                  ALU.subtract, ALU.mult), inc="VE")
            dq_state["last_stt"] = P.c("VE")
            dqkv = dq_state["last_stt"]
            for kvhh in range(2):
                pb = kvhh
                if si < 4:
                    kvh = si * 2 + kvhh
                    P.wait("tensor", "VE", dqkv)
                    P.wait("tensor", "VE", psum_free.get(pb, 0))
                    for g in range(G):
                        for hf in range(2):
                            P.emit("tensor", lambda E, g=g, kvhh=kvhh, hf=hf, pb=pb, wb=wb:
                                   E.matmul(kvp[pb][:, hf * 512:(hf + 1) * 512],
                                            wslab[wb][:, g * SLW + kvhh * 128: g * SLW + (kvhh + 1) * 128],
                                            xn1[:, g * T + hf * 512: g * T + (hf + 1) * 512],
                                            start=(g == 0), stop=(g == G - 1)),
                                   inc="PE" if (g == G - 1 and hf == 1) else None)
                    mmk = P.c("PE")
                    P.wait("scalar", "PE", mmk)
                    P.wait("scalar", "PE", kraw_pe)
                    ack = P.emit("scalar", lambda E, pb=pb: E.activation(kraw[:], kvp[pb][:], AF.Copy), inc="AC")
                    P.wait("tensor", "AC", ack)
                    P.wait("tensor", "VE", jk_free)
                    for hf in range(2):
                        P.emit("tensor", lambda E, hf=hf:
                               E.matmul(jkp[:, hf * 512:(hf + 1) * 512], jmat[:],
                                        kraw[:, hf * 512:(hf + 1) * 512], start=True, stop=True),
                               inc="PE" if hf == 1 else None)
                    pej = P.c("PE")
                    kraw_pe = pej
                    P.wait("vector", "LD", ld_ropek)
                    P.wait("vector", "PE", pej)
                    P.emit("vector", lambda E, pb=pb: E.tensor_mul(stg[0][:], kvp[pb][:], kocc[:]))
                    jk_free = P.emit("vector", lambda E: E.tensor_mul(stg[1][:], jkp[:], koss[:]), inc="VE")
                    psum_free[pb] = jk_free
                    vek = P.emit("vector", lambda E, kvh=kvh: E.tensor_add(kstg[kvh][:], stg[0][:], stg[1][:]), inc="VE")
                    stg_ve = vek
                    P.wait("scalar", "VE", vek)
                    acs = P.emit("scalar", lambda E, kvh=kvh: E.activation(ks2[:], kstg[kvh][:], AF.Square), inc="AC")
                    P.wait("tensor", "AC", acs)
                    P.wait("tensor", "VE", kn_ve)
                    for hf in range(2):
                        P.emit("tensor", lambda E, hf=hf:
                               E.matmul(kn2p[0:1, hf * 512:(hf + 1) * 512], consts[:, 0:1],
                                        ks2[:, hf * 512:(hf + 1) * 512], start=True, stop=True),
                               inc="PE" if hf == 1 else None)
                    pkn = P.c("PE")
                    P.wait("vector", "PE", pkn)
                    P.emit("vector", lambda E, kvh=kvh: E.reduce_max(mk2row[:, kvh:kvh + 1], kn2p[:], AX.X))
                    kn_ve = P.emit("vector", lambda E: E.tensor_copy(jnk[:], ident[:, 0:64]), inc="VE")
                    P.wait("gpsimd", "VE", vek)
                    P.emit("gpsimd", lambda E, kvh=kvh:
                           E.dma_start(out=k_own[kvh * 8:(kvh + 1) * 8, :].rearrange("kc (d k) -> d kc k", k=128),
                                       in_=kstg[kvh][:].rearrange("d (kc k) -> d kc k", k=128)),
                           inc="ST", n=16)
                else:
                    kvh = (si - 4) * 2 + kvhh
                    P.wait("tensor", "VE", dqkv)
                    P.wait("tensor", "AC", vraw_ac.get(pb, 0))
                    P.wait("tensor", "VE", psum_free.get(pb, 0))
                    for g in range(G):
                        for hf in range(2):
                            P.emit("tensor", lambda E, g=g, kvhh=kvhh, hf=hf, pb=pb, wb=wb:
                                   E.matmul(kvp[pb][:, hf * 512:(hf + 1) * 512],
                                            wslab[wb][:, g * SLW + kvhh * 128: g * SLW + (kvhh + 1) * 128],
                                            xn1[:, g * T + hf * 512: g * T + (hf + 1) * 512],
                                            start=(g == 0), stop=(g == G - 1)),
                                   inc="PE" if (g == G - 1 and hf == 1) else None)
                    mmv = P.c("PE")
                    P.wait("scalar", "PE", mmv)
                    P.wait("scalar", "PE", kraw_pe)
                    acv = P.emit("scalar", lambda E, pb=pb: E.activation(kraw[:], kvp[pb][:], AF.Copy), inc="AC")
                    vraw_ac[pb] = acv
                    P.wait("tensor", "AC", acv)
                    P.wait("tensor", "AC", vraw_ac.get(("stg", pb), 0))
                    P.wait("tensor", "VE", kn_ve)
                    for j in range(8):
                        P.emit("tensor", lambda E, j=j:
                               E.transpose(vtp[:, j * 128:(j + 1) * 128], kraw[:, j * 128:(j + 1) * 128], ident[:]),
                               inc="PE" if j == 7 else None)
                    ptr = P.c("PE")
                    kraw_pe = ptr
                    P.wait("scalar", "PE", ptr)
                    act = P.emit("scalar", lambda E, kvh=kvh: E.activation(vstg[kvh][:], vtp[:], AF.Copy), inc="AC")
                    vraw_ac[("stg", kvh % 2)] = act
                    P.wait("gpsimd", "AC", act)
                    P.emit("gpsimd", lambda E, kvh=kvh:
                           E.dma_start(out=v_own[kvh * 8:(kvh + 1) * 8, :].rearrange("kc (p c) -> p kc c", c=128),
                                       in_=vstg[kvh][:].rearrange("p (kc c) -> p kc c", c=128)),
                           inc="ST", n=16)
            slab_free_wb[wb] = P.c("PE")
        P.wait("gpsimd", "VE", P.c("VE"))
        P.emit("gpsimd", lambda E: E.dma_start(out=mk_own, in_=mk2row[:]), inc="ST", n=16)
        P.wait("gpsimd", "ST", P.c("ST"))
        P.emit("gpsimd", lambda E: E.collective_compute(
            "AllGather", ALU.bypass, replica_groups=[[0, 1, 2, 3], [4, 5, 6, 7]],
            ins=[k_own.opt()], outs=[k_gath.opt()]), inc="CC")
        P.emit("gpsimd", lambda E: E.collective_compute(
            "AllGather", ALU.bypass, replica_groups=[[0, 1, 2, 3], [4, 5, 6, 7]],
            ins=[v_own.opt()], outs=[v_gath.opt()]), inc="CC")
        P.emit("gpsimd", lambda E: E.collective_compute(
            "AllReduce", ALU.max, replica_groups=[[0, 1, 2, 3], [4, 5, 6, 7]],
            ins=[mk_own.opt()], outs=[mk_red.opt()]), inc="CC")
        P.barrier()

        qcc = V(ACT3, 2 * T, T)
        qss = V(ACT3, 3 * T, T)
        qraw = V(ACT3, 4 * T, T)
        sqq = V(ACT3, 5 * T, T)
        qstg = [V(BFS, 0, T), V(BFS, T, T)]
        qpp = [V(PA, 0, T), V(PB, 0, T)]
        jqp = V(PC, 0, T)
        qn2p = V(PD, 0, T, parts=1)
        c1 = P.emit("sync", lambda E: E.dma_start(out=qcc[:], in_=ropeq_cc), inc="LD", n=16)
        c2 = P.emit("sync", lambda E: E.dma_start(out=qss[:], in_=ropeq_ss), inc="LD", n=16)
        ld_rope = P.c("LD")
        wsl_free = {}
        psum_free = {}
        jq_free = 0
        qraw_pe = 0
        sq_pe = 0
        qn_ve = 0
        qstg_st = {}
        qstg_ac = {}
        for s in range(NSL_Q):
            wb = s % 2
            dq = dequant_slab(wslab[wb], qcod, s, schq, s * SLW,
                              ("PE", wsl_free.get(wb, 0)))
            for m in range(2):
                h = s * 2 + m
                hb = h % 2
                P.wait("tensor", "VE", dq)
                P.wait("tensor", "VE", psum_free.get(hb, 0))
                for g in range(G):
                    for hf in range(2):
                        P.emit("tensor", lambda E, g=g, m=m, hf=hf, wb=wb, hb=hb:
                               E.matmul(qpp[hb][:, hf * 512:(hf + 1) * 512],
                                        wslab[wb][:, g * SLW + m * 128: g * SLW + (m + 1) * 128],
                                        xn1[:, g * T + hf * 512: g * T + (hf + 1) * 512],
                                        start=(g == 0), stop=(g == G - 1)),
                               inc="PE" if (g == G - 1 and hf == 1) else None)
                mm = P.c("PE")
                P.wait("scalar", "PE", mm)
                P.wait("scalar", "PE", qraw_pe)
                ac1 = P.emit("scalar", lambda E, hb=hb: E.activation(qraw[:], qpp[hb][:], AF.Copy), inc="AC")
                P.wait("tensor", "AC", ac1)
                P.wait("tensor", "VE", jq_free)
                for hf in range(2):
                    P.emit("tensor", lambda E, hf=hf:
                           E.matmul(jqp[:, hf * 512:(hf + 1) * 512], jmat[:],
                                    qraw[:, hf * 512:(hf + 1) * 512], start=True, stop=True),
                           inc="PE" if hf == 1 else None)
                pe_j = P.c("PE")
                qraw_pe = pe_j
                P.wait("vector", "LD", ld_rope)
                P.wait("vector", "PE", pe_j)
                P.emit("vector", lambda E, hb=hb: E.tensor_mul(stg[0][:], qpp[hb][:], qcc[:]))
                jq_free = P.emit("vector", lambda E: E.tensor_mul(stg[1][:], jqp[:], qss[:]), inc="VE")
                psum_free[hb] = jq_free
                P.wait("vector", "ST", qstg_st.get(hb, 0))
                P.wait("vector", "AC", qstg_ac.get(hb, 0))
                veq = P.emit("vector", lambda E, hb=hb: E.tensor_add(qstg[hb][:], stg[0][:], stg[1][:]), inc="VE")
                P.wait("gpsimd", "VE", veq)
                qstg_st[hb] = P.emit("gpsimd", lambda E, h=h, hb=hb: E.dma_start(out=q_scr[h], in_=qstg[hb][:]), inc="ST", n=16)
                P.wait("scalar", "VE", veq)
                P.wait("scalar", "PE", sq_pe)
                acq = P.emit("scalar", lambda E, hb=hb: E.activation(sqq[:], qstg[hb][:], AF.Square), inc="AC")
                qstg_ac[hb] = acq
                P.wait("tensor", "AC", acq)
                P.wait("tensor", "VE", qn_ve)
                for hf in range(2):
                    P.emit("tensor", lambda E, hf=hf:
                           E.matmul(qn2p[0:1, hf * 512:(hf + 1) * 512], consts[:, 0:1], sqq[:, hf * 512:(hf + 1) * 512], start=True, stop=True),
                           inc="PE" if hf == 1 else None)
                sq_pe = P.c("PE")
                P.wait("vector", "PE", sq_pe)
                qn_ve = P.emit("vector", lambda E, h=h: E.reduce_max(mq2row[:, h:h + 1], qn2p[:], AX.X), inc="VE")
                vflush()
            wsl_free[wb] = P.c("PE")
        P.barrier()

        P.wait("gpsimd", "CC", 3)
        cmk = P.emit("gpsimd", lambda E: E.dma_start(out=mk2row[:], in_=mk_red), inc="LD2", n=16)
        P.wait("vector", "LD2", cmk)


        nmp = V(PA, 0, 32)
        vflush()
        mk2rep = bass.AP(rowsB.ap().tensor, 128, [[320, 1], [1, 8], [0, 4]])
        prod3 = prod[:].rearrange("a (b c) -> a b c", c=4)
        mq3 = mq2row[:].rearrange("a (b c) -> a b c", c=4)
        P.emit("vector", lambda E: E.tensor_mul(prod3, mq3, mk2rep))
        vp = P.emit("vector", lambda E: E.tensor_copy(jnk[0:1, 0:8], ident[0:1, 0:8]), inc="VE")
        P.wait("scalar", "VE", vp)
        P.emit("scalar", lambda E: E.activation(prod[:], prod[:], AF.Ln))
        aM = P.emit("scalar", lambda E: E.activation(prod[:], prod[:], AF.Exp, scale=0.5), inc="AC")
        P.wait("vector", "AC", aM)
        vM = P.emit("vector", lambda E: E.tensor_scalar(negm[:], prod[:], -1.0, None, ALU.mult), inc="VE")
        P.wait("tensor", "VE", vM)
        pM = P.emit("tensor", lambda E: E.matmul(nmp[:], onesrow[:], negm[:], start=True, stop=True), inc="PE")
        P.wait("vector", "PE", pM)
        P.emit("vector", lambda E: E.tensor_copy(negmb[:], nmp[:]), inc="VE")
        P.barrier()

        attnA = V(ACT1, 0, H * T)
        kh = V(KVW, 0, S)
        vh = V(KVW, S, S)
        qh = [V(KVW, 2 * S, T), V(KVW, 2 * S + T, T)]
        PT = [V(BFS, 0, T), V(BFS, T, T)]
        PTm = [V(BFS, 2 * T, T), V(BFS, 3 * T, T)]
        spsum = [V(PA, 0, T), V(PB, 0, T)]
        pvp = V(PC, 0, T)
        denp = V(PD, 0, T, parts=1)
        Ppair = [V(KVW, 2 * S + 2 * T, T), V(KVW, 2 * S + 3 * T, T)]
        Pquad = [V(KVW, 2 * S + 4 * T, T), V(KVW, 2 * S + 5 * T, T)]
        ppair_ve = {}
        kv_free = 0
        qh_ld = {}
        exp_done = {}
        ptm_free = {}
        bias_free = {}
        pvp_free = 0
        denp_free = 0
        denstage_free = {}
        cq = P.emit("sync", lambda E: E.dma_start(
            out=stg[0][:].rearrange("p (a t) -> p a t", a=1),
            in_=kqpos.partition_broadcast(128)), inc="LD", n=16)
        ck = P.emit("sync", lambda E: E.dma_start(out=RD[:, 0:32], in_=kpcol), inc="LD", n=16)
        P.wait("vector", "LD", P.c("LD"))
        for kvh in range(KVH):
            P.wait("gpsimd", "PE", kv_free)
            for s in range(4):
                P.emit("gpsimd", lambda E, kvh=kvh, s=s: E.dma_start(
                    out=kh[:, s * T:(s + 1) * T].rearrange("d (kc k) -> d kc k", k=128),
                    in_=k_gath[s * 64 + kvh * 8: s * 64 + kvh * 8 + 8, :].rearrange("kc (d k) -> d kc k", k=128)),
                    inc="LD2", n=16)
            kld = P.c("LD2")
            for s in range(4):
                P.emit("gpsimd", lambda E, kvh=kvh, s=s: E.dma_start(
                    out=vh[:, s * T:(s + 1) * T].rearrange("p (kc c) -> p kc c", c=128),
                    in_=v_gath[s * 64 + kvh * 8: s * 64 + kvh * 8 + 8, :].rearrange("kc (p c) -> p kc c", c=128)),
                    inc="LD2", n=16)
            vld = P.c("LD2")
            for hh in range(4):
                h = kvh * 4 + hh
                hb = h % 2
                P.wait("gpsimd", "PE", qh_ld.get(("free", hb), 0))
                P.emit("gpsimd", lambda E, h=h, hb=hb: E.dma_start(out=qh[hb][:], in_=q_scr[h]), inc="LD2", n=16)
                qh_ld[h] = P.c("LD2")
                P.wait("vector", "AC", bias_free.get(hb, 0))
                vb = P.emit("vector", lambda E, hb=hb, h=h:
                            E.tensor_scalar(biasH[hb][:], biascol[:], negmb[:, h:h + 1], None, ALU.add), inc="VE")
                for kc in range(33):
                    if kc < 32:
                        sbk = kc % 2
                        if kc >= 2:
                            P.wait("tensor", "AC", exp_done.get((h, kc - 2), 0))
                        else:
                            P.wait("tensor", "AC", exp_done.get((h - 1, 30 + sbk), 0))
                        if kc == 0:
                            P.wait("tensor", "LD2", kld)
                            P.wait("tensor", "LD2", qh_ld[h])
                        for hf in range(2):
                            P.emit("tensor", lambda E, kc=kc, h=h, hf=hf, sbk=sbk, hb=hb:
                                   E.matmul(spsum[sbk][:, hf * 512:(hf + 1) * 512],
                                            kh[:, kc * 128:(kc + 1) * 128],
                                            qh[hb][:, hf * 512:(hf + 1) * 512],
                                            start=True, stop=True),
                                   inc="PE" if hf == 1 else None)
                        sc_mm = P.c("PE")
                        P.wait("scalar", "PE", sc_mm)
                        P.wait("scalar", "VE", vb)
                        if kc >= 2:
                            P.wait("scalar", "PE", exp_done.get(("pvdone", h, kc - 2), 0))
                        elif h > 0:
                            P.wait("scalar", "PE", exp_done.get(("pvdone", h - 1, 30 + sbk), 0))
                        exp_done[(h, kc)] = P.emit("scalar", lambda E, kc=kc, sbk=sbk, hb=hb:
                                                   E.activation(PT[sbk][:], spsum[sbk][:], AF.Exp, bias=biasH[hb][:, kc:kc + 1]),
                                                   inc="AC")
                        P.wait("vector", "AC", exp_done[(h, kc)])
                        P.wait("vector", "PE", ptm_free.get(sbk, 0))
                        exp_done[("mask", h, kc)] = P.emit("vector", lambda E, kc=kc, sbk=sbk:
                                                           E.scalar_tensor_tensor(PTm[sbk][:], stg[0][:], RD[:, kc:kc + 1],
                                                                                  PT[sbk][:], ALU.is_ge, ALU.mult),
                                                           inc="VE")
                        if kc % 2 == 1:
                            pj = kc // 2
                            pbuf = pj % 2
                            ppair_ve[pj] = P.emit("vector", lambda E, pbuf=pbuf:
                                                  E.tensor_add(Ppair[pbuf][:], PTm[0][:], PTm[1][:]), inc="VE")
                            if pj % 2 == 1:
                                qj = pj // 2
                                qb = qj % 2
                                P.wait("vector", "PE", ppair_ve.get(("qfree", qb), 0))
                                ppair_ve[("q", qj)] = P.emit("vector", lambda E, qb=qb:
                                                             E.tensor_add(Pquad[qb][:], Ppair[0][:], Ppair[1][:]), inc="VE")
                    if kc >= 1:
                        kcp = kc - 1
                        sbp = kcp % 2
                        src = PTm[sbp]
                        P.wait("tensor", "VE", exp_done[("mask", h, kcp)])
                        if kcp == 0:
                            P.wait("tensor", "LD2", vld)
                            P.wait("tensor", "AC", pvp_free)
                            P.wait("tensor", "VE", denp_free)
                        for hf in range(2):
                            P.emit("tensor", lambda E, kcp=kcp, src=src, hf=hf:
                                   E.matmul(pvp[:, hf * 512:(hf + 1) * 512],
                                            vh[:, kcp * 128:(kcp + 1) * 128],
                                            src[:, hf * 512:(hf + 1) * 512],
                                            start=(kcp == 0), stop=(kcp == 31)),
                                   inc="PE" if (hf == 1 and kcp % 4 != 3) else None)
                        if kcp % 4 == 3:
                            qj = kcp // 4
                            qb = qj % 2
                            P.wait("tensor", "VE", ppair_ve[("q", qj)])
                            if qj == 0:
                                P.wait("tensor", "VE", denp_free)
                            for hf in range(2):
                                P.emit("tensor", lambda E, qb=qb, hf=hf, qj=qj:
                                       E.matmul(denp[0:1, hf * 512:(hf + 1) * 512], onescol_bf[:],
                                                Pquad[qb][:, hf * 512:(hf + 1) * 512],
                                                start=(qj == 0), stop=(qj == 7)),
                                       inc="PE" if hf == 1 else None)
                            ppair_ve[("qfree", qb)] = P.c("PE")
                        exp_done[("pvdone", h, kcp)] = P.c("PE")
                        ptm_free[sbp] = P.c("PE")
                pv_mm = P.c("PE")
                qh_ld[("free", hb)] = pv_mm
                P.wait("scalar", "PE", pv_mm)
                pvp_free = P.emit("scalar", lambda E, h=h: E.activation(attnA[:, h * T:(h + 1) * T], pvp[:], AF.Copy), inc="AC")
                bias_free[hb] = pvp_free
                P.wait("vector", "PE", pv_mm)
                P.wait("vector", "ST", denstage_free.get(hb, 0))
                denp_free = P.emit("vector", lambda E, hb=hb: E.tensor_copy(denstage[hb][:], denp[0:1, :]), inc="VE")
                P.wait("gpsimd", "VE", denp_free)
                denstage_free[hb] = P.emit("gpsimd", lambda E, h=h, hb=hb:
                                           E.dma_start(out=MB[h:h + 1, 0:T], in_=denstage[hb][:]), inc="ST", n=16)
            kv_free = P.c("PE")
        ac_attn = P.c("AC")
        P.barrier()

        rb_sb = [V(ACT3, 4 * T, T), V(ACT3, 5 * T, T)]
        rbp = V(PA, 0, T)
        ve_rd = P.emit("vector", lambda E: E.reciprocal(rd32[:], dens32[:]), inc="VE")
        vflush()
        rstage_free = {}
        rb_free = {}
        rbp_free = 0
        for h in range(H):
            hb = h % 2
            P.wait("gpsimd", "VE", ve_rd)
            P.wait("gpsimd", "PE", rstage_free.get(hb, 0))
            g1 = P.emit("gpsimd", lambda E, h=h, hb=hb: E.dma_start(out=rstage[hb][:], in_=RD[h:h + 1, 0:T]), inc="ST", n=16)
            P.wait("tensor", "ST", g1)
            P.wait("tensor", "VE", rbp_free)
            for hf in range(2):
                P.emit("tensor", lambda E, hb=hb, hf=hf:
                       E.matmul(rbp[:, hf * 512:(hf + 1) * 512], onesrow[:], rstage[hb][0:1, hf * 512:(hf + 1) * 512], start=True, stop=True),
                       inc="PE" if hf == 1 else None)
            rstage_free[hb] = P.c("PE")
            P.wait("vector", "PE", rstage_free[hb])
            P.wait("vector", "VE", rb_free.get(hb, 0))
            rbp_free = P.emit("vector", lambda E, hb=hb: E.tensor_copy(rb_sb[hb][:], rbp[:]), inc="VE")
            P.wait("vector", "AC", ac_attn)
            rb_free[hb] = P.emit("vector", lambda E, h=h, hb=hb:
                                 E.scalar_tensor_tensor(attnA[:, h * T:(h + 1) * T], attnA[:, h * T:(h + 1) * T],
                                                        1.0, rb_sb[hb][:], ALU.bypass, ALU.mult), inc="VE")
        P.barrier()

        xb4 = [V(ACT3, 2 * T, T), V(ACT3, 3 * T, T)]
        x1s = [V(ACT3, 4 * T, T), V(ACT3, 5 * T, T)]
        sq4 = V(ACT3, 6 * T, T)
        opp = [V(PA, 0, T), V(PB, 0, T)]
        mp2 = V(PC, 0, T, parts=1)
        s2p2 = V(PD, 0, T, parts=1)
        wsl_free = {}
        psum_free = {}
        xb4_ve = {}
        x1s_pe = {}
        x1s_st = {}
        sq_free = 0
        for s in range(NSL_Q):
            wb = s % 2
            dq = dequant_slab(wslab[wb], ocod, s, scho, s * SLW,
                              ("PE", wsl_free.get(wb, 0)))
            for m in range(2):
                mt = s * 2 + m
                mb = mt % 2
                P.wait("sync", "VE", xb4_ve.get(mb, 0))
                cx = P.emit("sync", lambda E, mt=mt, mb=mb: E.dma_start(out=xb4[mb][:], in_=xq_t[mt * 128:(mt + 1) * 128, :]), inc="LD", n=16)
                P.wait("tensor", "VE", dq)
                P.wait("tensor", "VE", psum_free.get(mb, 0))
                for g in range(G):
                    for hf in range(2):
                        P.emit("tensor", lambda E, g=g, m=m, hf=hf, wb=wb, mb=mb:
                               E.matmul(opp[mb][:, hf * 512:(hf + 1) * 512],
                                        wslab[wb][:, g * SLW + m * 128: g * SLW + (m + 1) * 128],
                                        attnA[:, g * T + hf * 512: g * T + (hf + 1) * 512],
                                        start=(g == 0), stop=(g == G - 1)),
                               inc="PE" if (g == G - 1 and hf == 1) else None)
                mm = P.c("PE")
                P.wait("vector", "PE", mm)
                P.wait("vector", "PE", x1s_pe.get(mb, 0))
                P.wait("vector", "LD", cx)
                P.wait("vector", "ST", x1s_st.get(mb, 0))
                vex1 = P.emit("vector", lambda E, mb=mb: E.tensor_add(x1s[mb][:], opp[mb][:], xb4[mb][:]), inc="VE")
                psum_free[mb] = vex1
                xb4_ve[mb] = vex1
                P.wait("gpsimd", "VE", vex1)
                x1s_st[mb] = P.emit("gpsimd", lambda E, mt=mt, mb=mb: E.dma_start(out=x1_scr[mt], in_=x1s[mb][:]), inc="ST", n=16)
                P.wait("scalar", "VE", vex1)
                P.wait("scalar", "PE", sq_free)
                acs = P.emit("scalar", lambda E, mb=mb: E.activation(sq4[:], x1s[mb][:], AF.Square), inc="AC")
                P.wait("tensor", "AC", acs)
                for hf in range(2):
                    P.emit("tensor", lambda E, mb=mb, hf=hf, mt=mt:
                           E.matmul(mp2[0:1, hf * 512:(hf + 1) * 512], consts[:, 2:3], x1s[mb][:, hf * 512:(hf + 1) * 512],
                                    start=(mt == 0), stop=(mt == G - 1)))
                    P.emit("tensor", lambda E, hf=hf, mt=mt:
                           E.matmul(s2p2[0:1, hf * 512:(hf + 1) * 512], consts[:, 2:3], sq4[:, hf * 512:(hf + 1) * 512],
                                    start=(mt == 0), stop=(mt == G - 1)),
                           inc="PE" if hf == 1 else None)
                sq_free = P.c("PE")
                x1s_pe[mb] = sq_free
            wsl_free[wb] = P.c("PE")
        pe_stats2 = P.c("PE")
        P.wait("vector", "PE", pe_stats2)
        P.emit("vector", lambda E: E.tensor_copy(ln_m[:], mp2[0:1, :]))
        P.emit("vector", lambda E: E.tensor_copy(ln_s2[:], s2p2[0:1, :]))
        P.emit("vector", lambda E: E.tensor_mul(stg[0][0:1, :], ln_m[:], ln_m[:]))
        v1 = P.emit("vector", lambda E: E.tensor_sub(ln_s2[:], ln_s2[:], stg[0][0:1, :]), inc="VE")
        P.wait("scalar", "VE", v1)
        P.emit("scalar", lambda E: E.activation(ln_s2[:], ln_s2[:], AF.Ln, bias=EPS))
        a2 = P.emit("scalar", lambda E: E.activation(ln_rstd[:], ln_s2[:], AF.Exp, scale=-0.5), inc="AC")
        P.wait("vector", "AC", a2)
        P.emit("vector", lambda E: E.tensor_copy(stg[1][0:1, :], ln_m[:]))
        v2 = P.emit("vector", lambda E: E.tensor_mul(ln_m[:], ln_m[:], ln_rstd[:]), inc="VE")
        P.barrier()

        rbp2 = V(PA, 0, T)
        mrbp2 = V(PB, 0, T)
        mbp2 = V(PC, 0, T)
        for hf in range(2):
            lo, hi = hf * 512, (hf + 1) * 512
            P.emit("tensor", lambda E, lo=lo, hi=hi: E.matmul(rbp2[:, lo:hi], onesrow[:], ln_rstd[0:1, lo:hi], start=True, stop=True))
            P.emit("tensor", lambda E, lo=lo, hi=hi: E.matmul(mrbp2[:, lo:hi], onesrow[:], ln_m[0:1, lo:hi], start=True, stop=True))
            P.emit("tensor", lambda E, lo=lo, hi=hi: E.matmul(mbp2[:, lo:hi], onesrow[:], stg[1][0:1, lo:hi], start=True, stop=True),
                   inc="PE" if hf == 1 else None)
        peb2 = P.c("PE")
        P.wait("vector", "PE", peb2)
        P.emit("vector", lambda E: E.tensor_copy(Rb[:], rbp2[:]))
        P.emit("vector", lambda E: E.tensor_scalar(nMRb[:], mrbp2[:], -1.0, None, ALU.mult))
        P.emit("vector", lambda E: E.tensor_copy(Mb2raw[:], mbp2[:]), inc="VE")
        P.barrier()

        xn2 = V(ACT1, 0, G * T)
        gateT = V(ACT3, 7 * T, T, parts=64)
        xb5 = [V(ACT3, 2 * T, T), V(ACT3, 3 * T, T)]
        wqw_sb = V(ACT3, 4 * T, 512)
        qrp = V(PA, 0, T, parts=16)
        wsp = V(PB, 0, 1, parts=16)
        cw = P.emit("sync", lambda E: E.dma_start(out=wqw_sb[:].rearrange("p (g n) -> p g n", n=16),
                                                  in_=wqw_t.rearrange("g p n -> p g n")), inc="LD", n=16)
        xfree = {}
        for g in range(G):
            i = g % 2
            P.wait("sync", "VE", xfree.get(i, 0))
            P.wait("sync", "PE", xfree.get(("pe", i), 0))
            c1 = P.emit("sync", lambda E, g=g, b=xb5[i]: E.dma_start(out=b[:], in_=x1_scr[g]), inc="LD", n=16)
            P.wait("vector", "LD", c1)
            P.emit("vector", lambda E, b=xb5[i]: E.tensor_mul(stg[0][:], b[:], Rb[:]))
            xfree[i] = P.emit("vector", lambda E, g=g: E.tensor_add(xn2[:, g * T:(g + 1) * T], stg[0][:], nMRb[:]), inc="VE")
            P.wait("tensor", "LD", max(c1, cw))
            for hf in range(2):
                P.emit("tensor", lambda E, g=g, i=i, hf=hf:
                       E.matmul(qrp[:, hf * 512:(hf + 1) * 512], wqw_sb[:, g * 16:(g + 1) * 16],
                                xb5[i][:, hf * 512:(hf + 1) * 512], start=(g == 0), stop=(g == G - 1)))
            P.emit("tensor", lambda E, g=g: E.matmul(wsp[:], wqw_sb[:, g * 16:(g + 1) * 16], consts[:, 1:2],
                                                     start=(g == 0), stop=(g == G - 1)),
                   inc="PE" if g == G - 1 else None)
            xfree[("pe", i)] = P.c("PE")
        pe_qr = P.c("PE")

        wsn_sb = V(ACT3, 6 * T + 512 + 64, 1, parts=16)
        t1 = V(ACT3, 4 * T + 512, T, parts=16)
        qr_sb = V(ACT3, 5 * T + 512, T, parts=16)
        keys_sb = V(ACT3, 6 * T + 512, 64, parts=16)
        scT = V(ACT3, 2 * T, T, parts=64)
        stok = V(ACT3, 3 * T, 512)
        sa2 = V(ACT3, 3 * T + 512, 64)
        mz = V(ACT3, 3 * T + 576, 64)
        m01 = V(ACT3, 3 * T + 640, 64)
        r16 = V(ACT3, 3 * T + 704, 16)
        e16 = V(ACT3, 3 * T + 720, 16)
        nrm = V(ACT3, 3 * T + 736, 1)
        den6 = V(ACT3, 3 * T + 737, 1)
        rden6 = V(ACT3, 3 * T + 738, 1)
        expv = V(ACT3, 3 * T + 752, 64)
        gtok = V(ACT3, 3 * T + 816, 64)
        scp = V(PC, 0, T, parts=64)
        tpp = V(PD, 0, 64)
        gtp = V(PD, 512, 128, parts=64)
        ck = P.emit("sync", lambda E: E.dma_start(out=keys_sb[:], in_=keys_t), inc="LD", n=16)
        P.wait("vector", "PE", pe_qr)
        P.emit("vector", lambda E: E.tensor_copy(wsn_sb[:], wsp[:]))
        vflush()
        P.emit("vector", lambda E: E.scalar_tensor_tensor(t1[:], Mb2raw[0:16, :], wsn_sb[:], qrp[:], ALU.mult, ALU.add))
        vqr = P.emit("vector", lambda E: E.tensor_mul(qr_sb[:], t1[:], Rb[0:16, :]), inc="VE")
        P.wait("tensor", "VE", vqr)
        P.wait("tensor", "LD", ck)
        for hf in range(2):
            P.emit("tensor", lambda E, hf=hf: E.matmul(scp[:, hf * 512:(hf + 1) * 512], keys_sb[:], qr_sb[:, hf * 512:(hf + 1) * 512], start=True, stop=True),
                   inc="PE" if hf == 1 else None)
        pesc = P.c("PE")
        P.wait("vector", "PE", pesc)
        vsc2 = P.emit("vector", lambda E: E.tensor_copy(scT[:], scp[:]), inc="VE")
        tp_free = 0
        gtp_free = 0
        gg_prev = 0
        for qt in range(8):
            P.wait("tensor", "VE", vsc2)
            P.wait("tensor", "VE", tp_free)
            ptp = P.emit("tensor", lambda E, qt=qt: E.transpose(tpp[:], scT[:, qt * 128:(qt + 1) * 128], ident[0:64, 0:64]), inc="PE")
            P.wait("vector", "PE", ptp)
            tp_free = P.emit("vector", lambda E, qt=qt: E.tensor_copy(stok[:, qt * 64:(qt + 1) * 64], tpp[:]), inc="VE")
            P.emit("vector", lambda E, qt=qt: E.max(out=r16[:, 0:8], in_=stok[:, qt * 64:(qt + 1) * 64]))
            vflush()
            P.emit("vector", lambda E, qt=qt: E.tensor_scalar(mz[:], stok[:, qt * 64:(qt + 1) * 64], r16[:, 7:8], None, ALU.is_ge))
            P.emit("vector", lambda E, qt=qt: E.scalar_tensor_tensor(sa2[:], mz[:], NEG, stok[:, qt * 64:(qt + 1) * 64], ALU.mult, ALU.add))
            P.emit("vector", lambda E: E.max(out=r16[:, 8:16], in_=sa2[:]))
            vflush()
            P.emit("vector", lambda E: E.tensor_scalar(nrm[:], r16[:, 0:1], -1.0, None, ALU.mult))
            vr = P.emit("vector", lambda E, qt=qt: E.tensor_scalar(m01[:], stok[:, qt * 64:(qt + 1) * 64], r16[:, 15:16], None, ALU.is_ge), inc="VE")
            P.wait("scalar", "VE", vr)
            P.emit("scalar", lambda E: E.activation(e16[:], r16[:], AF.Exp, bias=nrm[:]))
            ae = P.emit("scalar", lambda E, qt=qt: E.activation(expv[:], stok[:, qt * 64:(qt + 1) * 64], AF.Exp, bias=nrm[:]), inc="AC")
            P.wait("vector", "AC", ae)
            P.emit("vector", lambda E: E.reduce_sum(den6[:], e16[:], AX.X))
            vflush()
            P.emit("vector", lambda E: E.reciprocal(rden6[:], den6[:]))
            vflush()
            P.wait("vector", "PE", gg_prev)
            vg = P.emit("vector", lambda E: E.scalar_tensor_tensor(gtok[:], expv[:], rden6[:], m01[:], ALU.mult, ALU.mult), inc="VE")
            P.wait("tensor", "VE", vg)
            gg = P.emit("tensor", lambda E: E.transpose(gtp[:], gtok[:], ident[:]), inc="PE")
            gg_prev = gg
            P.wait("vector", "PE", gg)
            gtp_free = P.emit("vector", lambda E, qt=qt: E.tensor_copy(gateT[:, qt * 128:(qt + 1) * 128], gtp[:]), inc="VE")
        P.barrier()

        awslab = V(ACT2, 0, G * 512)
        zg = V(BFS, 0, 4 * T)
        grb = V(ACT3, 2 * T, T)
        afb = [V(ACT3, 3 * T, 512), V(ACT3, 3 * T + 512, 512)]
        rrep_sb = V(ACT3, 4 * T, 512, parts=64)
        zp = [V(PA, 0, T), V(PB, 0, T)]
        grp = V(PC, 0, T)
        cr = P.emit("sync", lambda E: E.dma_start(out=rrep_sb[:].rearrange("p (m n) -> p m n", n=128),
                                                  in_=rrep_i.rearrange("m p n -> p m n")), inc="LD", n=16)
        afree = {}
        for g in range(G):
            i = g % 2
            P.wait("sync", "VE", afree.get(i, 0))
            c1 = P.emit("sync", lambda E, g=g, b=afb[i]: E.dma_start(out=b[:], in_=aw_t[g]), inc="LD", n=16)
            P.wait("vector", "LD", c1)
            afree[i] = P.emit("vector", lambda E, g=g, b=afb[i]: E.tensor_copy(awslab[:, g * 512:(g + 1) * 512], b[:]), inc="VE")
        ve_aw = P.c("VE")
        zp_free = {}
        grb_free = 0
        for m in range(4):
            mb = m % 2
            P.wait("tensor", "LD", cr)
            P.wait("tensor", "VE", grb_free)
            for hf in range(2):
                P.emit("tensor", lambda E, m=m, hf=hf:
                       E.matmul(grp[:, hf * 512:(hf + 1) * 512], rrep_sb[:, m * 128:(m + 1) * 128],
                                gateT[:, hf * 512:(hf + 1) * 512], start=True, stop=True),
                       inc="PE" if hf == 1 else None)
            peg = P.c("PE")
            P.wait("vector", "PE", peg)
            veg = P.emit("vector", lambda E: E.tensor_copy(grb[:], grp[:]), inc="VE")
            grb_free = veg
            P.wait("tensor", "VE", ve_aw)
            P.wait("tensor", "VE", zp_free.get(mb, 0))
            for g in range(G):
                for hf in range(2):
                    P.emit("tensor", lambda E, g=g, m=m, hf=hf, mb=mb:
                           E.matmul(zp[mb][:, hf * 512:(hf + 1) * 512],
                                    awslab[:, g * 512 + m * 128: g * 512 + (m + 1) * 128],
                                    xn2[:, g * T + hf * 512: g * T + (hf + 1) * 512],
                                    start=(g == 0), stop=(g == G - 1)),
                           inc="PE" if (g == G - 1 and hf == 1) else None)
            pez = P.c("PE")
            P.wait("vector", "PE", pez)
            P.wait("vector", "VE", veg)
            zp_free[mb] = P.emit("vector", lambda E, m=m, mb=mb:
                                 E.scalar_tensor_tensor(zg[:, m * T:(m + 1) * T], zp[mb][:], 1.0, grb[:], ALU.bypass, ALU.mult), inc="VE")
        ve_zg = P.c("VE")
        if debug:
            P.wait("gpsimd", "VE", P.c("VE"))
            for nm, src in [("d_dens", dens32), ("d_negm", negmb), ("d_gate", gateT),
                            ("d_zg", zg), ("d_xn2", xn2)]:
                if nm in dbg:
                    P.emit("gpsimd", lambda E, d=dbg[nm], s=src: E.dma_start(out=d, in_=s[:]), inc="ST", n=16)
        P.barrier()

        bw = V(KVW, 0, 4 * 512)
        bfb = [V(ACT3, 2 * T, 512), V(ACT3, 2 * T + 512, 512)]
        xb7 = [V(ACT3, 3 * T, T), V(ACT3, 4 * T, T)]
        ost = [V(ACT3, 5 * T, T), V(ACT3, 6 * T, T)]
        bp = [V(PA, 0, T), V(PB, 0, T)]
        bfree = {}
        bw_free = 0
        bp_free = {}
        ost_free = {}
        for s in range(8):
            for g in range(4):
                i = g % 2
                P.wait("sync", "VE", bfree.get(i, 0))
                c1 = P.emit("sync", lambda E, s=s, g=g, b=bfb[i]: E.dma_start(out=b[:], in_=bw_t[s, g]), inc="LD", n=16)
                P.wait("vector", "LD", c1)
                if g == 0:
                    P.wait("vector", "PE", bw_free)
                bfree[i] = P.emit("vector", lambda E, g=g, b=bfb[i]:
                                  E.tensor_scalar(bw[:, g * 512:(g + 1) * 512], b[:], 2.0, None, ALU.mult), inc="VE")
            ve_bw = P.c("VE")
            for m in range(4):
                mt = s * 4 + m
                mb = mt % 2
                P.wait("sync", "VE", ost_free.get(("x", mb), 0))
                cx = P.emit("sync", lambda E, mt=mt, mb=mb: E.dma_start(out=xb7[mb][:], in_=x1_scr[mt]), inc="LD", n=16)
                P.wait("tensor", "VE", ve_bw)
                P.wait("tensor", "VE", ve_zg)
                P.wait("tensor", "VE", bp_free.get(mb, 0))
                for g in range(4):
                    for hf in range(2):
                        P.emit("tensor", lambda E, g=g, m=m, hf=hf, mb=mb:
                               E.matmul(bp[mb][:, hf * 512:(hf + 1) * 512],
                                        bw[:, g * 512 + m * 128: g * 512 + (m + 1) * 128],
                                        zg[:, g * T + hf * 512: g * T + (hf + 1) * 512],
                                        start=(g == 0), stop=(g == 3)),
                               inc="PE" if (g == 3 and hf == 1) else None)
                pem = P.c("PE")
                P.wait("vector", "PE", pem)
                P.wait("vector", "LD", cx)
                P.wait("vector", "ST", ost_free.get(mb, 0))
                bp_free[mb] = P.emit("vector", lambda E, mb=mb: E.tensor_add(ost[mb][:], bp[mb][:], xb7[mb][:]), inc="VE")
                ost_free[("x", mb)] = bp_free[mb]
                P.wait("gpsimd", "VE", bp_free[mb])
                ost_free[mb] = P.emit("gpsimd", lambda E, mt=mt, mb=mb: E.dma_start(out=out_t[mt], in_=ost[mb][:]), inc="ST", n=16)
            bw_free = P.c("PE")

        P.wait("gpsimd", "ST", P.c("ST"))

        block = stack.enter_context(nc.Block())

        def mk(engname):
            def f(eng):
                for fn in P.q[engname]:
                    fn(eng)
            return f
        block.sync(mk("sync"))
        block.gpsimd(mk("gpsimd"))
        block.vector(mk("vector"))
        block.scalar(mk("scalar"))
        block.tensor(mk("tensor"))

    return nc


def check_deadlock(P):
    pos = {e: 0 for e in ENGINES}
    cnt = {sname: 0 for sname in SEMS}
    total = sum(len(P.trace[e]) for e in ENGINES)
    done = 0
    progress = True
    while progress:
        progress = False
        for e in ENGINES:
            tr = P.trace[e]
            while pos[e] < len(tr):
                it = tr[pos[e]]
                if it[0] == "wait" and cnt[it[1]] < it[2]:
                    break
                if it[0] == "inc":
                    cnt[it[1]] += it[2]
                pos[e] += 1
                done += 1
                progress = True
    stuck = {e: (pos[e], len(P.trace[e]), P.trace[e][pos[e]] if pos[e] < len(P.trace[e]) else None) for e in ENGINES}
    return done == total, stuck, cnt


def _prep_codes(codes, nslab):
    return np.ascontiguousarray(
        codes.reshape(32, 128, nslab, 256).transpose(2, 1, 0, 3)).astype(np.int8)


def _prep_inputs(L0_attn_x, q_codes, k_codes, v_codes, o_codes,
                 q_scales, k_scales, v_scales, o_scales,
                 A_w, B_w, Wq_w, keys):
    f32 = np.float32
    shared = {}
    shared["qcod"] = _prep_codes(q_codes, 16)
    shared["ocod"] = _prep_codes(o_codes, 16)
    shared["kcod"] = _prep_codes(k_codes, 4)
    shared["vcod"] = _prep_codes(v_codes, 4)
    shared["qsc"] = np.ascontiguousarray(q_scales, f32)
    shared["osc"] = np.ascontiguousarray(o_scales, f32)
    shared["ksc"] = np.ascontiguousarray(k_scales, f32)
    shared["vsc"] = np.ascontiguousarray(v_scales, f32)
    shared["aw_t"] = np.ascontiguousarray(A_w.T.reshape(32, 128, 512), f32)
    shared["bw_t"] = np.ascontiguousarray(
        B_w.T.reshape(512, 8, 512).transpose(1, 0, 2).reshape(8, 4, 128, 512), f32)
    shared["wqw_t"] = np.ascontiguousarray(Wq_w.T.reshape(32, 128, 16), f32)
    shared["keys_t"] = np.ascontiguousarray(keys.T, f32)

    jm = np.zeros((128, 128), f32)
    for d in range(64):
        jm[d + 64, d] = 1.0
        jm[d, d + 64] = 1.0
    shared["jmat_i"] = jm
    shared["ident_i"] = np.eye(128, dtype=f32)
    rr = np.zeros((4, 64, 128), f32)
    for m in range(4):
        for nn in range(128):
            rr[m, (m * 128 + nn) // 8, nn] = 1.0
    shared["rrep_i"] = rr
    consts = np.zeros((128, 8), f32)
    consts[:, 0] = 1.0
    consts[:, 1] = -1.0
    consts[:, 2] = 1.0 / 4096.0
    shared["consts_i"] = consts

    def rope_tables(pos, scale):
        half = 64
        inv = 1.0 / (10000.0 ** (np.arange(half, dtype=np.float64) / half))
        ang = pos.astype(np.float64)[None, :] * inv[:, None]
        cc = np.concatenate([np.cos(ang), np.cos(ang)], axis=0) * scale
        ss = np.concatenate([-np.sin(ang), np.sin(ang)], axis=0) * scale
        return np.ascontiguousarray(cc, f32), np.ascontiguousarray(ss, f32)

    in_maps = []
    sd = 1.0 / np.sqrt(128.0)
    for c in range(8):
        b, c4 = c // 4, c % 4
        d = dict(shared)
        d["xq_t"] = np.ascontiguousarray(L0_attn_x[b, c4 * 1024:(c4 + 1) * 1024, :].T, f32)
        qpos = np.arange(c4 * 1024, (c4 + 1) * 1024)
        d["ropeq_cc"], d["ropeq_ss"] = rope_tables(qpos, sd)
        d["ropeko_cc"], d["ropeko_ss"] = rope_tables(qpos, 1.0)
        d["kqpos"] = np.ascontiguousarray(qpos[None, :], f32)
        kp = np.arange(32)[None, :] * 128 + np.arange(128)[:, None]
        d["kpcol"] = np.ascontiguousarray(kp, f32)
        bias = np.zeros((128, 32), f32)
        bias[:, (c4 + 1) * 8:] = NEG
        d["biascol_i"] = bias
        in_maps.append(d)
    return in_maps


def kernel(L0_attn_x, L0_attn_q_codes, L0_attn_k_codes, L0_attn_v_codes, L0_attn_o_codes,
           L0_attn_q_scales, L0_attn_k_scales, L0_attn_v_scales, L0_attn_o_scales,
           L0_ffn_A_w, L0_ffn_B_w, L0_ffn_Wq_w, L0_ffn_keys):
    in_maps = _prep_inputs(np.asarray(L0_attn_x), np.asarray(L0_attn_q_codes),
                           np.asarray(L0_attn_k_codes), np.asarray(L0_attn_v_codes),
                           np.asarray(L0_attn_o_codes), np.asarray(L0_attn_q_scales),
                           np.asarray(L0_attn_k_scales), np.asarray(L0_attn_v_scales),
                           np.asarray(L0_attn_o_scales), np.asarray(L0_ffn_A_w),
                           np.asarray(L0_ffn_B_w), np.asarray(L0_ffn_Wq_w),
                           np.asarray(L0_ffn_keys))
    nc = build_program(debug=False)
    res = run_bass_kernel_spmd(nc, in_maps, core_ids=list(range(8)))
    out = np.empty((2, 4096, 4096), np.float32)
    for c in range(8):
        b, c4 = c // 4, c % 4
        o = res.results[c]["out_t"].reshape(4096, 1024)
        out[b, c4 * 1024:(c4 + 1) * 1024, :] = o.T
    return out


# version 12
# speedup vs baseline: 1.0644x; 1.0644x over previous
import numpy as np
from contextlib import ExitStack

import concourse.bass as bass
import concourse.mybir as mybir
from concourse.bass_utils import run_bass_kernel_spmd

F32 = mybir.dt.float32
BF16 = mybir.dt.bfloat16
I8 = mybir.dt.int8
ALU = mybir.AluOpType
AF = mybir.ActivationFunctionType
AX = mybir.AxisListType

NF = 4096
G = 32
T = 1024
S = 4096
H = 32
KVH = 8
SLW = 256
NSL_Q = NF // SLW
KGS = 512
NKG = S // KGS
NEG = -1e30
EPS = 1e-5

ENGINES = ("sync", "gpsimd", "vector", "scalar", "tensor")
SEMS = ("LD", "LD2", "PE", "VE", "AC", "GP", "ST", "CC")


class Plan:
    def __init__(self, nc, stack):
        self.nc = nc
        self.q = {e: [] for e in ENGINES}
        self.trace = {e: [] for e in ENGINES}
        self.maxwait = {}
        self.sems = {}
        self.cnt = {}
        for name in SEMS:
            self.sems[name] = stack.enter_context(nc.semaphore(name))
            self.cnt[name] = 0

    def emit(self, eng, f, inc=None, n=1):
        if inc is not None:
            self.cnt[inc] += n
            sem = self.sems[inc]
            self.q[eng].append(lambda E, f=f, sem=sem, n=n: f(E).then_inc(sem, n))
            self.trace[eng].append(("inc", inc, n))
            return self.cnt[inc]
        self.q[eng].append(f)
        self.trace[eng].append(("op",))
        return None

    def wait(self, eng, semname, count):
        if count is None or count <= 0:
            return
        key = (eng, semname)
        if self.maxwait.get(key, -1) >= count:
            return
        self.maxwait[key] = count
        sem = self.sems[semname]
        self.q[eng].append(lambda E, sem=sem, c=count: E.wait_ge(sem, c))
        self.trace[eng].append(("wait", semname, count))

    def c(self, semname):
        return self.cnt[semname]

    def barrier(self):
        snap = dict(self.cnt)
        for e in ENGINES:
            for s in SEMS:
                if s == "CC":
                    continue
                self.wait(e, s, snap[s])


class V:
    def __init__(self, h, c0, cols, p0=0, parts=128):
        self.h, self.c0, self.cols, self.p0, self.parts = h, c0, cols, p0, parts

    def _pc(self, idx):
        if idx == slice(None):
            return (self.p0, self.p0 + self.parts, self.c0, self.c0 + self.cols)
        p, c = idx
        ps = self.p0 + (p.start or 0)
        pe = self.p0 + (self.parts if p.stop is None else p.stop)
        cs = self.c0 + (c.start or 0)
        ce = self.c0 + (self.cols if c.stop is None else c.stop)
        return (ps, pe, cs, ce)

    def __getitem__(self, idx):
        ps, pe, cs, ce = self._pc(idx)
        return self.h[ps:pe, cs:ce]


def build_program(debug=False):
    nc = bass.Bass("TRN2", target_bir_lowering=False)

    ap = lambda t: t.ap()
    xq_t = ap(nc.dram_tensor("xq_t", [NF, T], F32, kind="ExternalInput"))
    qcod = ap(nc.dram_tensor("qcod", [NSL_Q, 128, G, SLW], I8, kind="ExternalInput"))
    ocod = ap(nc.dram_tensor("ocod", [NSL_Q, 128, G, SLW], I8, kind="ExternalInput"))
    kcod = ap(nc.dram_tensor("kcod", [4, 128, G, SLW], I8, kind="ExternalInput"))
    vcod = ap(nc.dram_tensor("vcod", [4, 128, G, SLW], I8, kind="ExternalInput"))
    qsc = ap(nc.dram_tensor("qsc", [G, NF], F32, kind="ExternalInput"))
    osc = ap(nc.dram_tensor("osc", [G, NF], F32, kind="ExternalInput"))
    ksc = ap(nc.dram_tensor("ksc", [G, 1024], F32, kind="ExternalInput"))
    vsc = ap(nc.dram_tensor("vsc", [G, 1024], F32, kind="ExternalInput"))
    aw_t = ap(nc.dram_tensor("aw_t", [G, 128, 512], F32, kind="ExternalInput"))
    bw_t = ap(nc.dram_tensor("bw_t", [8, 4, 128, 512], F32, kind="ExternalInput"))
    wqw_t = ap(nc.dram_tensor("wqw_t", [G, 128, 16], F32, kind="ExternalInput"))
    keys_t = ap(nc.dram_tensor("keys_t", [16, 64], F32, kind="ExternalInput"))
    ropeq_cc = ap(nc.dram_tensor("ropeq_cc", [128, T], F32, kind="ExternalInput"))
    ropeq_ss = ap(nc.dram_tensor("ropeq_ss", [128, T], F32, kind="ExternalInput"))
    ropeko_cc = ap(nc.dram_tensor("ropeko_cc", [128, T], F32, kind="ExternalInput"))
    ropeko_ss = ap(nc.dram_tensor("ropeko_ss", [128, T], F32, kind="ExternalInput"))
    kqpos = ap(nc.dram_tensor("kqpos", [1, T], F32, kind="ExternalInput"))
    kpcol = ap(nc.dram_tensor("kpcol", [128, 32], F32, kind="ExternalInput"))
    biascol_i = ap(nc.dram_tensor("biascol_i", [128, 32], F32, kind="ExternalInput"))
    jmat_i = ap(nc.dram_tensor("jmat_i", [128, 128], F32, kind="ExternalInput"))
    ident_i = ap(nc.dram_tensor("ident_i", [128, 128], F32, kind="ExternalInput"))
    rrep_i = ap(nc.dram_tensor("rrep_i", [4, 64, 128], F32, kind="ExternalInput"))
    consts_i = ap(nc.dram_tensor("consts_i", [128, 8], F32, kind="ExternalInput"))

    out_t = ap(nc.dram_tensor("out_t", [G, 128, T], F32, kind="ExternalOutput"))

    kind_s = "ExternalOutput" if debug else "Internal"
    wkv_scr = ap(nc.dram_tensor("wkv_scr", [8, G, 128, SLW], BF16, kind="Internal"))
    q_scr = ap(nc.dram_tensor("q_scr", [H, 128, T], BF16, kind=kind_s))
    k_own_t = nc.dram_tensor("k_own", [KVH * 8, 128 * 128], BF16, kind="Internal")
    v_own_t = nc.dram_tensor("v_own", [KVH * 8, 128 * 128], BF16, kind="Internal")
    k_gath_t = nc.dram_tensor("k_gath", [4 * KVH * 8, 128 * 128], BF16, kind="Internal")
    v_gath_t = nc.dram_tensor("v_gath", [4 * KVH * 8, 128 * 128], BF16, kind="Internal")
    mk_own_t = nc.dram_tensor("mk_own", [1, 8], F32, kind="Internal")
    mk_red_t = nc.dram_tensor("mk_red", [1, 8], F32, kind="Internal")
    k_own, v_own = ap(k_own_t), ap(v_own_t)
    k_gath, v_gath = ap(k_gath_t), ap(v_gath_t)
    mk_own, mk_red = ap(mk_own_t), ap(mk_red_t)
    x1_scr = ap(nc.dram_tensor("x1_scr", [G, 128, T], F32, kind=kind_s))
    schq = ap(nc.dram_tensor("schq", [G, NF], BF16, kind="Internal"))
    scho = ap(nc.dram_tensor("scho", [G, NF], BF16, kind="Internal"))
    schk = ap(nc.dram_tensor("schk", [G, 1024], BF16, kind="Internal"))
    schv = ap(nc.dram_tensor("schv", [G, 1024], BF16, kind="Internal"))
    dbg = {}
    if debug:
        for nm, sh, dt in [("d_xn1", [128, G * T], BF16), ("d_dens", [32, T], F32),
                           ("d_negm", [128, 32], F32), ("d_gate", [64, T], F32),
                           ("d_zg", [128, 4 * T], BF16), ("d_xn2", [128, G * T], BF16)]:
            dbg[nm] = ap(nc.dram_tensor(nm, sh, dt, kind="ExternalOutput"))

    with ExitStack() as stack:
        P = Plan(nc, stack)
        sb = lambda name, shape, dt: stack.enter_context(nc.sbuf_tensor(name, shape, dt))

        consts = sb("consts", [128, 8], F32)
        onesrow = sb("onesrow", [1, 128], F32)
        onescol_bf = sb("onescol_bf", [128, 1], BF16)
        jmat = sb("jmat", [128, 128], F32)
        ident = sb("ident", [128, 128], F32)
        biascol = sb("biascol", [128, 32], F32)
        negmb = sb("negmb", [128, 32], F32)
        smallA = sb("smallA", [128, 80], F32)
        rowsA = sb("rowsA", [1, 3 * T], F32)
        rowsB = sb("rowsB", [1, 320], F32)
        RBt = sb("RBt", [128, 2 * T], F32)
        MB = sb("MB", [128, T], F32)
        RD = sb("RD", [128, T], F32)
        jnk = sb("jnk", [128, 64], F32)
        ceps = sb("ceps", [128, 1], F32)
        nc.const_aps.register(F32, EPS, ceps.ap())
        cods8 = sb("cods8", [128, G * SLW], I8)
        ACT1 = sb("ACT1", [128, G * T], BF16)
        ACT2 = sb("ACT2", [128, G * 512], BF16)
        KVW = sb("KVW", [128, G * 512], BF16)
        BFS = sb("BFS", [128, 4096], BF16)
        ACT3 = sb("ACT3", [128, 8192], F32)
        PA = stack.enter_context(nc.psum_tensor("PA", [128, T], F32))
        PB = stack.enter_context(nc.psum_tensor("PB", [128, T], F32))
        PC = stack.enter_context(nc.psum_tensor("PC", [128, T], F32))
        PD = stack.enter_context(nc.psum_tensor("PD", [128, T], F32))

        ln_m = V(rowsA, 0, T, parts=1)
        ln_s2 = V(rowsA, T, T, parts=1)
        ln_rstd = V(rowsA, 2 * T, T, parts=1)
        Rb = V(RBt, 0, T)
        nMRb = V(RBt, T, T)
        Mb2raw = V(MB, 0, T)
        dens32 = V(MB, 0, T, parts=32)
        rd32 = V(RD, 0, T, parts=32)
        denstage = [V(ACT3, 2 * T, T, parts=1), V(ACT3, 3 * T, T, parts=1)]
        rstage = denstage
        mkr = V(rowsB, 0, 1, parts=1)
        prod = V(rowsB, 32, 32, parts=1)
        negm = V(rowsB, 64, 32, parts=1)
        biasH = [V(smallA, 0, 32), V(smallA, 32, 32)]
        stg = [V(ACT3, 0, T), V(ACT3, T, T)]
        sbcs = V(KVW, 0, G * SLW)

        def vflush():
            P.emit("vector", lambda E: E.tensor_copy(jnk[:], ident[:, 0:64]))

        for apin, dst in [(consts_i, consts), (jmat_i, jmat), (ident_i, ident),
                          (biascol_i, biascol)]:
            P.emit("sync", lambda E, a=apin, d=dst: E.dma_start(out=d[:], in_=a), inc="LD", n=16)
        P.wait("vector", "LD", P.c("LD"))
        P.emit("vector", lambda E: E.memset(onesrow[:], 1.0))
        P.emit("vector", lambda E: E.memset(onescol_bf[:], 1.0))
        P.emit("vector", lambda E: E.memset(jnk[:], 0.0))
        P.emit("vector", lambda E: E.memset(ceps[:], EPS))

        mq2row = V(rowsB, 96, 32, parts=1)
        mk2row = V(rowsB, 128, 8, parts=1)
        ve_init = P.emit("vector", lambda E: E.memset(mk2row[:], -1e30), inc="VE")

        scf = V(ACT3, 2 * T, NF, parts=32)
        sch_b = V(ACT2, 0, NF, parts=32)
        for nmx, src, dst, n in [("q", qsc, schq, NF), ("o", osc, scho, NF),
                                 ("k", ksc, schk, 1024), ("v", vsc, schv, 1024)]:
            P.wait("sync", "VE", P.c("VE"))
            c0 = P.emit("sync", lambda E, s=src, n=n: E.dma_start(out=scf[0:32, 0:n], in_=s), inc="LD", n=16)
            P.wait("vector", "LD", c0)
            P.wait("vector", "ST", P.c("ST"))
            v0 = P.emit("vector", lambda E, n=n: E.tensor_scalar(sch_b[0:32, 0:n], scf[0:32, 0:n], 0.5, None, ALU.mult), inc="VE")
            P.wait("gpsimd", "VE", v0)
            P.emit("gpsimd", lambda E, d=dst, n=n: E.dma_start(out=d, in_=sch_b[0:32, 0:n]), inc="ST", n=16)
        P.barrier()

        dq_state = {"last_stt": 0}

        def dequant_slab(wslab, codes_src, s_idx, sch, col0, free_gate):
            P.wait("sync", "VE", dq_state["last_stt"])
            c1 = P.emit("sync", lambda E, a=codes_src, s=s_idx:
                        E.dma_start(out=cods8[:].rearrange("p (g n) -> p g n", n=SLW), in_=a[s]),
                        inc="LD", n=16)
            c2 = P.emit("sync", lambda E, a=sch, c0=col0:
                        E.dma_start(out=sbcs[:].rearrange("p (g n) -> p g n", n=SLW),
                                    in_=a[0:G, c0:c0 + SLW].partition_broadcast(128)),
                        inc="LD2", n=16)
            P.wait("vector", "LD", c1)
            P.wait("vector", "LD2", c2)
            if free_gate is not None:
                P.wait("vector", free_gate[0], free_gate[1])
            for g in range(G):
                P.emit("vector", lambda E, d=wslab, g=g:
                       E.scalar_tensor_tensor(d[:, g * SLW:(g + 1) * SLW], cods8[:, g * SLW:(g + 1) * SLW], 8.0,
                                              sbcs[:, g * SLW:(g + 1) * SLW], ALU.subtract, ALU.mult), inc="VE")
            dq_state["last_stt"] = P.c("VE")
            return dq_state["last_stt"]

        lnxb = [V(ACT3, 2 * T, T), V(ACT3, 3 * T, T)]
        lnsq = V(ACT3, 4 * T, T)
        LN = {"xfree": {}, "sqfree": 0}

        def ln_stats_and_bcast(xsrc_fn, ncols, mp, s2p, rbp, mrbp):
            xb, sq = lnxb, lnsq
            nh = (ncols + 511) // 512
            for g in range(G):
                i = g % 2
                P.wait("sync", "PE", LN["xfree"].get(i, 0))
                c1 = P.emit("sync", lambda E, a=xsrc_fn(g), b=xb[i], n=ncols:
                            E.dma_start(out=b[:, 0:n], in_=a), inc="LD", n=16)
                P.wait("scalar", "LD", c1)
                P.wait("scalar", "PE", LN["sqfree"])
                a1 = P.emit("scalar", lambda E, b=xb[i], n=ncols:
                            E.activation(sq[:, 0:n], b[:, 0:n], AF.Square), inc="AC")
                P.wait("tensor", "LD", c1)
                P.wait("tensor", "AC", a1)
                for hf in range(nh):
                    lo, hi = hf * 512, min((hf + 1) * 512, ncols)
                    P.emit("tensor", lambda E, b=xb[i], lo=lo, hi=hi, g=g:
                           E.matmul(mp[0:1, lo:hi], consts[:, 2:3], b[:, lo:hi],
                                    start=(g == 0), stop=(g == G - 1)))
                    P.emit("tensor", lambda E, lo=lo, hi=hi, g=g:
                           E.matmul(s2p[0:1, lo:hi], consts[:, 2:3], sq[:, lo:hi],
                                    start=(g == 0), stop=(g == G - 1)),
                           inc="PE" if (hf == nh - 1) else None)
                LN["xfree"][i] = P.c("PE")
                LN["sqfree"] = P.c("PE")
            pe_stats = P.c("PE")
            P.wait("vector", "PE", pe_stats)
            P.emit("vector", lambda E, n=ncols: E.tensor_copy(ln_m[:, 0:n], mp[0:1, 0:n]))
            P.emit("vector", lambda E, n=ncols: E.tensor_copy(ln_s2[:, 0:n], s2p[0:1, 0:n]))
            P.emit("vector", lambda E, n=ncols: E.tensor_mul(stg[0][0:1, 0:n], ln_m[:, 0:n], ln_m[:, 0:n]))
            v1 = P.emit("vector", lambda E, n=ncols: E.tensor_sub(ln_s2[:, 0:n], ln_s2[:, 0:n], stg[0][0:1, 0:n]), inc="VE")
            P.wait("scalar", "VE", v1)
            P.emit("scalar", lambda E, n=ncols: E.activation(ln_s2[:, 0:n], ln_s2[:, 0:n], AF.Ln, bias=EPS))
            a2 = P.emit("scalar", lambda E, n=ncols: E.activation(ln_rstd[:, 0:n], ln_s2[:, 0:n], AF.Exp, scale=-0.5), inc="AC")
            P.wait("vector", "AC", a2)
            v2 = P.emit("vector", lambda E, n=ncols: E.tensor_mul(ln_m[:, 0:n], ln_m[:, 0:n], ln_rstd[:, 0:n]), inc="VE")
            P.wait("tensor", "AC", a2)
            P.wait("tensor", "VE", v2)
            for hf in range(nh):
                lo, hi = hf * 512, min((hf + 1) * 512, ncols)
                P.emit("tensor", lambda E, lo=lo, hi=hi: E.matmul(rbp[:, lo:hi], onesrow[:], ln_rstd[0:1, lo:hi], start=True, stop=True))
                P.emit("tensor", lambda E, lo=lo, hi=hi: E.matmul(mrbp[:, lo:hi], onesrow[:], ln_m[0:1, lo:hi], start=True, stop=True),
                       inc="PE" if hf == nh - 1 else None)
            pe_b = P.c("PE")
            P.wait("vector", "PE", pe_b)
            P.emit("vector", lambda E, n=ncols: E.tensor_copy(Rb[:, 0:n], rbp[:, 0:n]))
            ve_done = P.emit("vector", lambda E, n=ncols: E.tensor_scalar(nMRb[:, 0:n], mrbp[:, 0:n], -1.0, None, ALU.mult), inc="VE")
            return ve_done

        xn1 = V(ACT1, 0, G * T)
        ve_ln1 = ln_stats_and_bcast(lambda g: xq_t[g * 128:(g + 1) * 128, :], T,
                                    V(PC, 0, T, parts=1), V(PD, 0, T, parts=1),
                                    V(PA, 0, T), V(PB, 0, T))
        for g in range(G):
            i = g % 2
            P.wait("sync", "VE", LN["xfree"].get(("n", i), 0))
            P.wait("sync", "PE", LN["xfree"].get(i, 0))
            c1 = P.emit("sync", lambda E, g=g, b=lnxb[i]: E.dma_start(out=b[:], in_=xq_t[g * 128:(g + 1) * 128, :]), inc="LD", n=16)
            P.wait("vector", "LD", c1)
            if g == 0:
                P.wait("vector", "VE", ve_ln1)
            P.emit("vector", lambda E, b=lnxb[i]: E.tensor_mul(stg[0][:], b[:], Rb[:]))
            LN["xfree"][("n", i)] = P.emit("vector", lambda E, g=g: E.tensor_add(xn1[:, g * T:(g + 1) * T], stg[0][:], nMRb[:]), inc="VE")
        if debug:
            P.wait("gpsimd", "VE", P.c("VE"))
            P.emit("gpsimd", lambda E: E.dma_start(out=dbg["d_xn1"], in_=xn1[:]), inc="ST", n=16)
        P.barrier()

        wslab = [V(ACT2, 0, G * SLW), V(ACT2, G * SLW, G * SLW)]

        kocc = V(ACT3, 2 * T, T)
        koss = V(ACT3, 3 * T, T)
        kraw = V(ACT3, 4 * T, T)
        ks2 = V(ACT3, 5 * T, T)
        kvw = V(ACT2, 0, G * SLW)
        kstg = [V(KVW, kvh * T, T) for kvh in range(KVH)]
        vstg = [V(KVW, 8192 + kvh * T, T) for kvh in range(KVH)]
        kvp = [V(PA, 0, T), V(PB, 0, T)]
        jkp = V(PC, 0, T)
        kn2p = V(PD, 0, T, parts=1)
        vtp = V(PD, 0, T)
        c1 = P.emit("sync", lambda E: E.dma_start(out=kocc[:], in_=ropeko_cc), inc="LD", n=16)
        c2 = P.emit("sync", lambda E: E.dma_start(out=koss[:], in_=ropeko_ss), inc="LD", n=16)
        ld_ropek = P.c("LD")
        sbcs_h = V(BFS, 0, 4096)
        slab_free_wb = {}
        kraw_pe = 0
        psum_free = {}
        jk_free = 0
        kn_ve = 0
        stg_ve = 0
        vraw_ac = {}
        for si in range(8):
            wb = si % 2
            cod = kcod if si < 4 else vcod
            sch = schk if si < 4 else schv
            sidx = si % 4
            P.wait("sync", "VE", dq_state["last_stt"])
            c1k = P.emit("sync", lambda E, a=cod, s=sidx:
                         E.dma_start(out=cods8[:].rearrange("p (g n) -> p g n", n=SLW), in_=a[s]),
                         inc="LD", n=16)
            P.wait("vector", "LD", c1k)
            P.wait("vector", "PE", slab_free_wb.get(wb, 0))
            for half in range(2):
                if half == 1:
                    P.wait("sync", "VE", P.c("VE"))
                c2k = P.emit("sync", lambda E, a=sch, c0=sidx * SLW, half=half:
                             E.dma_start(out=sbcs_h[:].rearrange("p (g n) -> p g n", n=SLW),
                                         in_=a[half * 16:(half + 1) * 16, c0:c0 + SLW].partition_broadcast(128)),
                             inc="LD2", n=16)
                P.wait("vector", "LD2", c2k)
                for g16 in range(16):
                    g = half * 16 + g16
                    P.emit("vector", lambda E, wb=wb, g=g, g16=g16:
                           E.scalar_tensor_tensor(wslab[wb][:, g * SLW:(g + 1) * SLW],
                                                  cods8[:, g * SLW:(g + 1) * SLW], 8.0,
                                                  sbcs_h[:, g16 * SLW:(g16 + 1) * SLW],
                                                  ALU.subtract, ALU.mult), inc="VE")
            dq_state["last_stt"] = P.c("VE")
            dqkv = dq_state["last_stt"]
            for kvhh in range(2):
                pb = kvhh
                if si < 4:
                    kvh = si * 2 + kvhh
                    P.wait("tensor", "VE", dqkv)
                    P.wait("tensor", "VE", psum_free.get(pb, 0))
                    for g in range(G):
                        for hf in range(2):
                            P.emit("tensor", lambda E, g=g, kvhh=kvhh, hf=hf, pb=pb, wb=wb:
                                   E.matmul(kvp[pb][:, hf * 512:(hf + 1) * 512],
                                            wslab[wb][:, g * SLW + kvhh * 128: g * SLW + (kvhh + 1) * 128],
                                            xn1[:, g * T + hf * 512: g * T + (hf + 1) * 512],
                                            start=(g == 0), stop=(g == G - 1)),
                                   inc="PE" if (g == G - 1 and hf == 1) else None)
                    mmk = P.c("PE")
                    P.wait("scalar", "PE", mmk)
                    P.wait("scalar", "PE", kraw_pe)
                    ack = P.emit("scalar", lambda E, pb=pb: E.activation(kraw[:], kvp[pb][:], AF.Copy), inc="AC")
                    P.wait("tensor", "AC", ack)
                    P.wait("tensor", "VE", jk_free)
                    for hf in range(2):
                        P.emit("tensor", lambda E, hf=hf:
                               E.matmul(jkp[:, hf * 512:(hf + 1) * 512], jmat[:],
                                        kraw[:, hf * 512:(hf + 1) * 512], start=True, stop=True),
                               inc="PE" if hf == 1 else None)
                    pej = P.c("PE")
                    kraw_pe = pej
                    P.wait("vector", "LD", ld_ropek)
                    P.wait("vector", "PE", pej)
                    P.emit("vector", lambda E, pb=pb: E.tensor_mul(stg[0][:], kvp[pb][:], kocc[:]))
                    jk_free = P.emit("vector", lambda E: E.tensor_mul(stg[1][:], jkp[:], koss[:]), inc="VE")
                    psum_free[pb] = jk_free
                    vek = P.emit("vector", lambda E, kvh=kvh: E.tensor_add(kstg[kvh][:], stg[0][:], stg[1][:]), inc="VE")
                    stg_ve = vek
                    P.wait("scalar", "VE", vek)
                    acs = P.emit("scalar", lambda E, kvh=kvh: E.activation(ks2[:], kstg[kvh][:], AF.Square), inc="AC")
                    P.wait("tensor", "AC", acs)
                    P.wait("tensor", "VE", kn_ve)
                    for hf in range(2):
                        P.emit("tensor", lambda E, hf=hf:
                               E.matmul(kn2p[0:1, hf * 512:(hf + 1) * 512], consts[:, 0:1],
                                        ks2[:, hf * 512:(hf + 1) * 512], start=True, stop=True),
                               inc="PE" if hf == 1 else None)
                    pkn = P.c("PE")
                    P.wait("vector", "PE", pkn)
                    P.emit("vector", lambda E, kvh=kvh: E.reduce_max(mk2row[:, kvh:kvh + 1], kn2p[:], AX.X))
                    kn_ve = P.emit("vector", lambda E: E.tensor_copy(jnk[:], ident[:, 0:64]), inc="VE")
                    P.wait("gpsimd", "VE", vek)
                    P.emit("gpsimd", lambda E, kvh=kvh:
                           E.dma_start(out=k_own[kvh * 8:(kvh + 1) * 8, :].rearrange("kc (d k) -> d kc k", k=128),
                                       in_=kstg[kvh][:].rearrange("d (kc k) -> d kc k", k=128)),
                           inc="ST", n=16)
                else:
                    kvh = (si - 4) * 2 + kvhh
                    P.wait("tensor", "VE", dqkv)
                    P.wait("tensor", "AC", vraw_ac.get(pb, 0))
                    P.wait("tensor", "VE", psum_free.get(pb, 0))
                    for g in range(G):
                        for hf in range(2):
                            P.emit("tensor", lambda E, g=g, kvhh=kvhh, hf=hf, pb=pb, wb=wb:
                                   E.matmul(kvp[pb][:, hf * 512:(hf + 1) * 512],
                                            wslab[wb][:, g * SLW + kvhh * 128: g * SLW + (kvhh + 1) * 128],
                                            xn1[:, g * T + hf * 512: g * T + (hf + 1) * 512],
                                            start=(g == 0), stop=(g == G - 1)),
                                   inc="PE" if (g == G - 1 and hf == 1) else None)
                    mmv = P.c("PE")
                    P.wait("scalar", "PE", mmv)
                    P.wait("scalar", "PE", kraw_pe)
                    acv = P.emit("scalar", lambda E, pb=pb: E.activation(kraw[:], kvp[pb][:], AF.Copy), inc="AC")
                    vraw_ac[pb] = acv
                    P.wait("tensor", "AC", acv)
                    P.wait("tensor", "AC", vraw_ac.get(("stg", pb), 0))
                    P.wait("tensor", "VE", kn_ve)
                    for j in range(8):
                        P.emit("tensor", lambda E, j=j:
                               E.transpose(vtp[:, j * 128:(j + 1) * 128], kraw[:, j * 128:(j + 1) * 128], ident[:]),
                               inc="PE" if j == 7 else None)
                    ptr = P.c("PE")
                    kraw_pe = ptr
                    P.wait("scalar", "PE", ptr)
                    act = P.emit("scalar", lambda E, kvh=kvh: E.activation(vstg[kvh][:], vtp[:], AF.Copy), inc="AC")
                    vraw_ac[("stg", kvh % 2)] = act
                    P.wait("gpsimd", "AC", act)
                    P.emit("gpsimd", lambda E, kvh=kvh:
                           E.dma_start(out=v_own[kvh * 8:(kvh + 1) * 8, :].rearrange("kc (p c) -> p kc c", c=128),
                                       in_=vstg[kvh][:].rearrange("p (kc c) -> p kc c", c=128)),
                           inc="ST", n=16)
            slab_free_wb[wb] = P.c("PE")
        P.wait("gpsimd", "VE", P.c("VE"))
        P.emit("gpsimd", lambda E: E.dma_start(out=mk_own, in_=mk2row[:]), inc="ST", n=16)
        P.wait("gpsimd", "ST", P.c("ST"))
        P.emit("gpsimd", lambda E: E.collective_compute(
            "AllGather", ALU.bypass, replica_groups=[[0, 1, 2, 3], [4, 5, 6, 7]],
            ins=[k_own.opt()], outs=[k_gath.opt()]), inc="CC")
        P.emit("gpsimd", lambda E: E.collective_compute(
            "AllGather", ALU.bypass, replica_groups=[[0, 1, 2, 3], [4, 5, 6, 7]],
            ins=[v_own.opt()], outs=[v_gath.opt()]), inc="CC")
        P.emit("gpsimd", lambda E: E.collective_compute(
            "AllReduce", ALU.max, replica_groups=[[0, 1, 2, 3], [4, 5, 6, 7]],
            ins=[mk_own.opt()], outs=[mk_red.opt()]), inc="CC")
        P.barrier()

        qcc = V(ACT3, 2 * T, T)
        qss = V(ACT3, 3 * T, T)
        qraw = V(ACT3, 4 * T, T)
        sqq = V(ACT3, 5 * T, T)
        qstg = [V(BFS, 0, T), V(BFS, T, T)]
        qpp = [V(PA, 0, T), V(PB, 0, T)]
        jqp = V(PC, 0, T)
        qn2p = V(PD, 0, T, parts=1)
        c1 = P.emit("sync", lambda E: E.dma_start(out=qcc[:], in_=ropeq_cc), inc="LD", n=16)
        c2 = P.emit("sync", lambda E: E.dma_start(out=qss[:], in_=ropeq_ss), inc="LD", n=16)
        ld_rope = P.c("LD")
        wsl_free = {}
        psum_free = {}
        jq_free = 0
        qraw_pe = 0
        sq_pe = 0
        qn_ve = 0
        qstg_st = {}
        qstg_ac = {}
        for s in range(NSL_Q):
            wb = s % 2
            dq = dequant_slab(wslab[wb], qcod, s, schq, s * SLW,
                              ("PE", wsl_free.get(wb, 0)))
            for m in range(2):
                h = s * 2 + m
                hb = h % 2
                P.wait("tensor", "VE", dq)
                P.wait("tensor", "VE", psum_free.get(hb, 0))
                for g in range(G):
                    for hf in range(2):
                        P.emit("tensor", lambda E, g=g, m=m, hf=hf, wb=wb, hb=hb:
                               E.matmul(qpp[hb][:, hf * 512:(hf + 1) * 512],
                                        wslab[wb][:, g * SLW + m * 128: g * SLW + (m + 1) * 128],
                                        xn1[:, g * T + hf * 512: g * T + (hf + 1) * 512],
                                        start=(g == 0), stop=(g == G - 1)),
                               inc="PE" if (g == G - 1 and hf == 1) else None)
                mm = P.c("PE")
                P.wait("scalar", "PE", mm)
                P.wait("scalar", "PE", qraw_pe)
                ac1 = P.emit("scalar", lambda E, hb=hb: E.activation(qraw[:], qpp[hb][:], AF.Copy), inc="AC")
                P.wait("tensor", "AC", ac1)
                P.wait("tensor", "VE", jq_free)
                for hf in range(2):
                    P.emit("tensor", lambda E, hf=hf:
                           E.matmul(jqp[:, hf * 512:(hf + 1) * 512], jmat[:],
                                    qraw[:, hf * 512:(hf + 1) * 512], start=True, stop=True),
                           inc="PE" if hf == 1 else None)
                pe_j = P.c("PE")
                qraw_pe = pe_j
                P.wait("vector", "LD", ld_rope)
                P.wait("vector", "PE", pe_j)
                P.emit("vector", lambda E, hb=hb: E.tensor_mul(stg[0][:], qpp[hb][:], qcc[:]))
                jq_free = P.emit("vector", lambda E: E.tensor_mul(stg[1][:], jqp[:], qss[:]), inc="VE")
                psum_free[hb] = jq_free
                P.wait("vector", "ST", qstg_st.get(hb, 0))
                P.wait("vector", "AC", qstg_ac.get(hb, 0))
                veq = P.emit("vector", lambda E, hb=hb: E.tensor_add(qstg[hb][:], stg[0][:], stg[1][:]), inc="VE")
                P.wait("gpsimd", "VE", veq)
                qstg_st[hb] = P.emit("gpsimd", lambda E, h=h, hb=hb: E.dma_start(out=q_scr[h], in_=qstg[hb][:]), inc="ST", n=16)
                P.wait("scalar", "VE", veq)
                P.wait("scalar", "PE", sq_pe)
                acq = P.emit("scalar", lambda E, hb=hb: E.activation(sqq[:], qstg[hb][:], AF.Square), inc="AC")
                qstg_ac[hb] = acq
                P.wait("tensor", "AC", acq)
                P.wait("tensor", "VE", qn_ve)
                for hf in range(2):
                    P.emit("tensor", lambda E, hf=hf:
                           E.matmul(qn2p[0:1, hf * 512:(hf + 1) * 512], consts[:, 0:1], sqq[:, hf * 512:(hf + 1) * 512], start=True, stop=True),
                           inc="PE" if hf == 1 else None)
                sq_pe = P.c("PE")
                P.wait("vector", "PE", sq_pe)
                qn_ve = P.emit("vector", lambda E, h=h: E.reduce_max(mq2row[:, h:h + 1], qn2p[:], AX.X), inc="VE")
                vflush()
            wsl_free[wb] = P.c("PE")
        P.barrier()

        P.wait("gpsimd", "CC", 3)
        cmk = P.emit("gpsimd", lambda E: E.dma_start(out=mk2row[:], in_=mk_red), inc="LD2", n=16)
        P.wait("vector", "LD2", cmk)


        nmp = V(PA, 0, 32)
        vflush()
        mk2rep = bass.AP(rowsB.ap().tensor, 128, [[320, 1], [1, 8], [0, 4]])
        prod3 = prod[:].rearrange("a (b c) -> a b c", c=4)
        mq3 = mq2row[:].rearrange("a (b c) -> a b c", c=4)
        P.emit("vector", lambda E: E.tensor_mul(prod3, mq3, mk2rep))
        vp = P.emit("vector", lambda E: E.tensor_copy(jnk[0:1, 0:8], ident[0:1, 0:8]), inc="VE")
        P.wait("scalar", "VE", vp)
        P.emit("scalar", lambda E: E.activation(prod[:], prod[:], AF.Ln))
        aM = P.emit("scalar", lambda E: E.activation(prod[:], prod[:], AF.Exp, scale=0.5), inc="AC")
        P.wait("vector", "AC", aM)
        vM = P.emit("vector", lambda E: E.tensor_scalar(negm[:], prod[:], -1.0, None, ALU.mult), inc="VE")
        P.wait("tensor", "VE", vM)
        pM = P.emit("tensor", lambda E: E.matmul(nmp[:], onesrow[:], negm[:], start=True, stop=True), inc="PE")
        P.wait("vector", "PE", pM)
        P.emit("vector", lambda E: E.tensor_copy(negmb[:], nmp[:]), inc="VE")
        P.barrier()

        attnA = V(ACT1, 0, H * T)
        kh = V(KVW, 0, S)
        vh = V(KVW, S, S)
        qh = [V(KVW, 2 * S, T), V(KVW, 2 * S + T, T)]
        PT = [V(BFS, 0, T), V(BFS, T, T)]
        PTm = [V(BFS, 2 * T, T), V(BFS, 3 * T, T)]
        spsum = [V(PA, 0, T), V(PB, 0, T)]
        pvp = V(PC, 0, T)
        denp = V(PD, 0, T, parts=1)
        Ppair = [V(KVW, 2 * S + 2 * T, T), V(KVW, 2 * S + 3 * T, T)]
        ppair_ve = {}
        kv_free = 0
        qh_ld = {}
        exp_done = {}
        ptm_free = {}
        bias_free = {}
        pvp_free = 0
        denp_free = 0
        denstage_free = {}
        cq = P.emit("sync", lambda E: E.dma_start(
            out=stg[0][:].rearrange("p (a t) -> p a t", a=1),
            in_=kqpos.partition_broadcast(128)), inc="LD", n=16)
        ck = P.emit("sync", lambda E: E.dma_start(out=RD[:, 0:32], in_=kpcol), inc="LD", n=16)
        P.wait("vector", "LD", P.c("LD"))
        for kvh in range(KVH):
            P.wait("gpsimd", "PE", kv_free)
            for s in range(4):
                P.emit("gpsimd", lambda E, kvh=kvh, s=s: E.dma_start(
                    out=kh[:, s * T:(s + 1) * T].rearrange("d (kc k) -> d kc k", k=128),
                    in_=k_gath[s * 64 + kvh * 8: s * 64 + kvh * 8 + 8, :].rearrange("kc (d k) -> d kc k", k=128)),
                    inc="LD2", n=16)
            kld = P.c("LD2")
            P.wait("sync", "PE", kv_free)
            for s in range(4):
                P.emit("sync", lambda E, kvh=kvh, s=s: E.dma_start(
                    out=vh[:, s * T:(s + 1) * T].rearrange("p (kc c) -> p kc c", c=128),
                    in_=v_gath[s * 64 + kvh * 8: s * 64 + kvh * 8 + 8, :].rearrange("kc (p c) -> p kc c", c=128)),
                    inc="LD2", n=16)
            vld = P.c("LD2")
            for hh in range(4):
                h = kvh * 4 + hh
                hb = h % 2
                P.wait("gpsimd", "PE", qh_ld.get(("free", hb), 0))
                P.emit("gpsimd", lambda E, h=h, hb=hb: E.dma_start(out=qh[hb][:], in_=q_scr[h]), inc="LD2", n=16)
                qh_ld[h] = P.c("LD2")
                P.wait("vector", "AC", bias_free.get(hb, 0))
                vb = P.emit("vector", lambda E, hb=hb, h=h:
                            E.tensor_scalar(biasH[hb][:], biascol[:], negmb[:, h:h + 1], None, ALU.add), inc="VE")
                for kc in range(33):
                    if kc < 32:
                        sbk = kc % 2
                        if kc >= 2:
                            P.wait("tensor", "AC", exp_done.get((h, kc - 2), 0))
                        else:
                            P.wait("tensor", "AC", exp_done.get((h - 1, 30 + sbk), 0))
                        if kc == 0:
                            P.wait("tensor", "LD2", kld)
                            P.wait("tensor", "LD2", qh_ld[h])
                        for hf in range(2):
                            P.emit("tensor", lambda E, kc=kc, h=h, hf=hf, sbk=sbk, hb=hb:
                                   E.matmul(spsum[sbk][:, hf * 512:(hf + 1) * 512],
                                            kh[:, kc * 128:(kc + 1) * 128],
                                            qh[hb][:, hf * 512:(hf + 1) * 512],
                                            start=True, stop=True),
                                   inc="PE" if hf == 1 else None)
                        sc_mm = P.c("PE")
                        P.wait("scalar", "PE", sc_mm)
                        P.wait("scalar", "VE", vb)
                        if kc >= 2:
                            P.wait("scalar", "PE", exp_done.get(("pvdone", h, kc - 2), 0))
                        elif h > 0:
                            P.wait("scalar", "PE", exp_done.get(("pvdone", h - 1, 30 + sbk), 0))
                        exp_done[(h, kc)] = P.emit("scalar", lambda E, kc=kc, sbk=sbk, hb=hb:
                                                   E.activation(PT[sbk][:], spsum[sbk][:], AF.Exp, bias=biasH[hb][:, kc:kc + 1]),
                                                   inc="AC")
                        P.wait("vector", "AC", exp_done[(h, kc)])
                        P.wait("vector", "PE", ptm_free.get(sbk, 0))
                        exp_done[("mask", h, kc)] = P.emit("vector", lambda E, kc=kc, sbk=sbk:
                                                           E.scalar_tensor_tensor(PTm[sbk][:], stg[0][:], RD[:, kc:kc + 1],
                                                                                  PT[sbk][:], ALU.is_ge, ALU.mult),
                                                           inc="VE")
                        if kc % 2 == 1:
                            pj = kc // 2
                            pbuf = pj % 2
                            P.wait("vector", "PE", ppair_ve.get(("free", pbuf), 0))
                            ppair_ve[pj] = P.emit("vector", lambda E, pbuf=pbuf:
                                                  E.tensor_add(Ppair[pbuf][:], PTm[0][:], PTm[1][:]), inc="VE")
                    if kc >= 1:
                        kcp = kc - 1
                        sbp = kcp % 2
                        src = PTm[sbp]
                        P.wait("tensor", "VE", exp_done[("mask", h, kcp)])
                        if kcp == 0:
                            P.wait("tensor", "LD2", vld)
                            P.wait("tensor", "AC", pvp_free)
                            P.wait("tensor", "VE", denp_free)
                        for hf in range(2):
                            P.emit("tensor", lambda E, kcp=kcp, src=src, hf=hf:
                                   E.matmul(pvp[:, hf * 512:(hf + 1) * 512],
                                            vh[:, kcp * 128:(kcp + 1) * 128],
                                            src[:, hf * 512:(hf + 1) * 512],
                                            start=(kcp == 0), stop=(kcp == 31)),
                                   inc="PE" if (hf == 1 and kcp % 2 == 0) else None)
                        if kcp % 2 == 1:
                            pj = kcp // 2
                            pbuf = pj % 2
                            P.wait("tensor", "VE", ppair_ve[pj])
                            if pj == 0:
                                P.wait("tensor", "VE", denp_free)
                            for hf in range(2):
                                P.emit("tensor", lambda E, pbuf=pbuf, hf=hf, pj=pj:
                                       E.matmul(denp[0:1, hf * 512:(hf + 1) * 512], onescol_bf[:],
                                                Ppair[pbuf][:, hf * 512:(hf + 1) * 512],
                                                start=(pj == 0), stop=(pj == 15)),
                                       inc="PE" if hf == 1 else None)
                            ppair_ve[("free", pbuf)] = P.c("PE")
                        exp_done[("pvdone", h, kcp)] = P.c("PE")
                        ptm_free[sbp] = P.c("PE")
                pv_mm = P.c("PE")
                qh_ld[("free", hb)] = pv_mm
                P.wait("scalar", "PE", pv_mm)
                pvp_free = P.emit("scalar", lambda E, h=h: E.activation(attnA[:, h * T:(h + 1) * T], pvp[:], AF.Copy), inc="AC")
                bias_free[hb] = pvp_free
                P.wait("vector", "PE", pv_mm)
                P.wait("vector", "ST", denstage_free.get(hb, 0))
                denp_free = P.emit("vector", lambda E, hb=hb: E.tensor_copy(denstage[hb][:], denp[0:1, :]), inc="VE")
                P.wait("gpsimd", "VE", denp_free)
                denstage_free[hb] = P.emit("gpsimd", lambda E, h=h, hb=hb:
                                           E.dma_start(out=MB[h:h + 1, 0:T], in_=denstage[hb][:]), inc="ST", n=16)
            kv_free = P.c("PE")
        ac_attn = P.c("AC")
        P.barrier()

        rb_sb = [V(ACT3, 4 * T, T), V(ACT3, 5 * T, T)]
        rbp = V(PA, 0, T)
        ve_rd = P.emit("vector", lambda E: E.reciprocal(rd32[:], dens32[:]), inc="VE")
        vflush()
        rstage_free = {}
        rb_free = {}
        rbp_free = 0
        for h in range(H):
            hb = h % 2
            P.wait("gpsimd", "VE", ve_rd)
            P.wait("gpsimd", "PE", rstage_free.get(hb, 0))
            g1 = P.emit("gpsimd", lambda E, h=h, hb=hb: E.dma_start(out=rstage[hb][:], in_=RD[h:h + 1, 0:T]), inc="ST", n=16)
            P.wait("tensor", "ST", g1)
            P.wait("tensor", "VE", rbp_free)
            for hf in range(2):
                P.emit("tensor", lambda E, hb=hb, hf=hf:
                       E.matmul(rbp[:, hf * 512:(hf + 1) * 512], onesrow[:], rstage[hb][0:1, hf * 512:(hf + 1) * 512], start=True, stop=True),
                       inc="PE" if hf == 1 else None)
            rstage_free[hb] = P.c("PE")
            P.wait("vector", "PE", rstage_free[hb])
            P.wait("vector", "VE", rb_free.get(hb, 0))
            rbp_free = P.emit("vector", lambda E, hb=hb: E.tensor_copy(rb_sb[hb][:], rbp[:]), inc="VE")
            P.wait("vector", "AC", ac_attn)
            rb_free[hb] = P.emit("vector", lambda E, h=h, hb=hb:
                                 E.scalar_tensor_tensor(attnA[:, h * T:(h + 1) * T], attnA[:, h * T:(h + 1) * T],
                                                        1.0, rb_sb[hb][:], ALU.bypass, ALU.mult), inc="VE")
        P.barrier()

        xb4 = [V(ACT3, 2 * T, T), V(ACT3, 3 * T, T)]
        x1s = [V(ACT3, 4 * T, T), V(ACT3, 5 * T, T)]
        sq4 = V(ACT3, 6 * T, T)
        opp = [V(PA, 0, T), V(PB, 0, T)]
        mp2 = V(PC, 0, T, parts=1)
        s2p2 = V(PD, 0, T, parts=1)
        wsl_free = {}
        psum_free = {}
        xb4_ve = {}
        x1s_pe = {}
        x1s_st = {}
        sq_free = 0
        for s in range(NSL_Q):
            wb = s % 2
            dq = dequant_slab(wslab[wb], ocod, s, scho, s * SLW,
                              ("PE", wsl_free.get(wb, 0)))
            for m in range(2):
                mt = s * 2 + m
                mb = mt % 2
                P.wait("sync", "VE", xb4_ve.get(mb, 0))
                cx = P.emit("sync", lambda E, mt=mt, mb=mb: E.dma_start(out=xb4[mb][:], in_=xq_t[mt * 128:(mt + 1) * 128, :]), inc="LD", n=16)
                P.wait("tensor", "VE", dq)
                P.wait("tensor", "VE", psum_free.get(mb, 0))
                for g in range(G):
                    for hf in range(2):
                        P.emit("tensor", lambda E, g=g, m=m, hf=hf, wb=wb, mb=mb:
                               E.matmul(opp[mb][:, hf * 512:(hf + 1) * 512],
                                        wslab[wb][:, g * SLW + m * 128: g * SLW + (m + 1) * 128],
                                        attnA[:, g * T + hf * 512: g * T + (hf + 1) * 512],
                                        start=(g == 0), stop=(g == G - 1)),
                               inc="PE" if (g == G - 1 and hf == 1) else None)
                mm = P.c("PE")
                P.wait("vector", "PE", mm)
                P.wait("vector", "PE", x1s_pe.get(mb, 0))
                P.wait("vector", "LD", cx)
                P.wait("vector", "ST", x1s_st.get(mb, 0))
                vex1 = P.emit("vector", lambda E, mb=mb: E.tensor_add(x1s[mb][:], opp[mb][:], xb4[mb][:]), inc="VE")
                psum_free[mb] = vex1
                xb4_ve[mb] = vex1
                P.wait("gpsimd", "VE", vex1)
                x1s_st[mb] = P.emit("gpsimd", lambda E, mt=mt, mb=mb: E.dma_start(out=x1_scr[mt], in_=x1s[mb][:]), inc="ST", n=16)
                P.wait("scalar", "VE", vex1)
                P.wait("scalar", "PE", sq_free)
                acs = P.emit("scalar", lambda E, mb=mb: E.activation(sq4[:], x1s[mb][:], AF.Square), inc="AC")
                P.wait("tensor", "AC", acs)
                for hf in range(2):
                    P.emit("tensor", lambda E, mb=mb, hf=hf, mt=mt:
                           E.matmul(mp2[0:1, hf * 512:(hf + 1) * 512], consts[:, 2:3], x1s[mb][:, hf * 512:(hf + 1) * 512],
                                    start=(mt == 0), stop=(mt == G - 1)))
                    P.emit("tensor", lambda E, hf=hf, mt=mt:
                           E.matmul(s2p2[0:1, hf * 512:(hf + 1) * 512], consts[:, 2:3], sq4[:, hf * 512:(hf + 1) * 512],
                                    start=(mt == 0), stop=(mt == G - 1)),
                           inc="PE" if hf == 1 else None)
                sq_free = P.c("PE")
                x1s_pe[mb] = sq_free
            wsl_free[wb] = P.c("PE")
        pe_stats2 = P.c("PE")
        P.wait("vector", "PE", pe_stats2)
        P.emit("vector", lambda E: E.tensor_copy(ln_m[:], mp2[0:1, :]))
        P.emit("vector", lambda E: E.tensor_copy(ln_s2[:], s2p2[0:1, :]))
        P.emit("vector", lambda E: E.tensor_mul(stg[0][0:1, :], ln_m[:], ln_m[:]))
        v1 = P.emit("vector", lambda E: E.tensor_sub(ln_s2[:], ln_s2[:], stg[0][0:1, :]), inc="VE")
        P.wait("scalar", "VE", v1)
        P.emit("scalar", lambda E: E.activation(ln_s2[:], ln_s2[:], AF.Ln, bias=EPS))
        a2 = P.emit("scalar", lambda E: E.activation(ln_rstd[:], ln_s2[:], AF.Exp, scale=-0.5), inc="AC")
        P.wait("vector", "AC", a2)
        P.emit("vector", lambda E: E.tensor_copy(stg[1][0:1, :], ln_m[:]))
        v2 = P.emit("vector", lambda E: E.tensor_mul(ln_m[:], ln_m[:], ln_rstd[:]), inc="VE")
        P.barrier()

        rbp2 = V(PA, 0, T)
        mrbp2 = V(PB, 0, T)
        mbp2 = V(PC, 0, T)
        for hf in range(2):
            lo, hi = hf * 512, (hf + 1) * 512
            P.emit("tensor", lambda E, lo=lo, hi=hi: E.matmul(rbp2[:, lo:hi], onesrow[:], ln_rstd[0:1, lo:hi], start=True, stop=True))
            P.emit("tensor", lambda E, lo=lo, hi=hi: E.matmul(mrbp2[:, lo:hi], onesrow[:], ln_m[0:1, lo:hi], start=True, stop=True))
            P.emit("tensor", lambda E, lo=lo, hi=hi: E.matmul(mbp2[:, lo:hi], onesrow[:], stg[1][0:1, lo:hi], start=True, stop=True),
                   inc="PE" if hf == 1 else None)
        peb2 = P.c("PE")
        P.wait("vector", "PE", peb2)
        P.emit("vector", lambda E: E.tensor_copy(Rb[:], rbp2[:]))
        P.emit("vector", lambda E: E.tensor_scalar(nMRb[:], mrbp2[:], -1.0, None, ALU.mult))
        P.emit("vector", lambda E: E.tensor_copy(Mb2raw[:], mbp2[:]), inc="VE")
        P.barrier()

        xn2 = V(ACT1, 0, G * T)
        gateT = V(ACT3, 7 * T, T, parts=64)
        xb5 = [V(ACT3, 2 * T, T), V(ACT3, 3 * T, T)]
        wqw_sb = V(ACT3, 4 * T, 512)
        qrp = V(PA, 0, T, parts=16)
        wsp = V(PB, 0, 1, parts=16)
        cw = P.emit("sync", lambda E: E.dma_start(out=wqw_sb[:].rearrange("p (g n) -> p g n", n=16),
                                                  in_=wqw_t.rearrange("g p n -> p g n")), inc="LD", n=16)
        xfree = {}
        for g in range(G):
            i = g % 2
            P.wait("sync", "VE", xfree.get(i, 0))
            P.wait("sync", "PE", xfree.get(("pe", i), 0))
            c1 = P.emit("sync", lambda E, g=g, b=xb5[i]: E.dma_start(out=b[:], in_=x1_scr[g]), inc="LD", n=16)
            P.wait("vector", "LD", c1)
            P.emit("vector", lambda E, b=xb5[i]: E.tensor_mul(stg[0][:], b[:], Rb[:]))
            xfree[i] = P.emit("vector", lambda E, g=g: E.tensor_add(xn2[:, g * T:(g + 1) * T], stg[0][:], nMRb[:]), inc="VE")
            P.wait("tensor", "LD", max(c1, cw))
            for hf in range(2):
                P.emit("tensor", lambda E, g=g, i=i, hf=hf:
                       E.matmul(qrp[:, hf * 512:(hf + 1) * 512], wqw_sb[:, g * 16:(g + 1) * 16],
                                xb5[i][:, hf * 512:(hf + 1) * 512], start=(g == 0), stop=(g == G - 1)))
            P.emit("tensor", lambda E, g=g: E.matmul(wsp[:], wqw_sb[:, g * 16:(g + 1) * 16], consts[:, 1:2],
                                                     start=(g == 0), stop=(g == G - 1)),
                   inc="PE" if g == G - 1 else None)
            xfree[("pe", i)] = P.c("PE")
        pe_qr = P.c("PE")

        wsn_sb = V(ACT3, 6 * T + 512 + 64, 1, parts=16)
        t1 = V(ACT3, 4 * T + 512, T, parts=16)
        qr_sb = V(ACT3, 5 * T + 512, T, parts=16)
        keys_sb = V(ACT3, 6 * T + 512, 64, parts=16)
        scT = V(ACT3, 2 * T, T, parts=64)
        stok = V(ACT3, 3 * T, 512)
        sa2 = V(ACT3, 3 * T + 512, 64)
        mz = V(ACT3, 3 * T + 576, 64)
        m01 = V(ACT3, 3 * T + 640, 64)
        r16 = V(ACT3, 3 * T + 704, 16)
        e16 = V(ACT3, 3 * T + 720, 16)
        nrm = V(ACT3, 3 * T + 736, 1)
        den6 = V(ACT3, 3 * T + 737, 1)
        rden6 = V(ACT3, 3 * T + 738, 1)
        expv = V(ACT3, 3 * T + 752, 64)
        gtok = V(ACT3, 3 * T + 816, 64)
        scp = V(PC, 0, T, parts=64)
        tpp = V(PD, 0, 64)
        gtp = V(PD, 512, 128, parts=64)
        ck = P.emit("sync", lambda E: E.dma_start(out=keys_sb[:], in_=keys_t), inc="LD", n=16)
        P.wait("vector", "PE", pe_qr)
        P.emit("vector", lambda E: E.tensor_copy(wsn_sb[:], wsp[:]))
        vflush()
        P.emit("vector", lambda E: E.scalar_tensor_tensor(t1[:], Mb2raw[0:16, :], wsn_sb[:], qrp[:], ALU.mult, ALU.add))
        vqr = P.emit("vector", lambda E: E.tensor_mul(qr_sb[:], t1[:], Rb[0:16, :]), inc="VE")
        P.wait("tensor", "VE", vqr)
        P.wait("tensor", "LD", ck)
        for hf in range(2):
            P.emit("tensor", lambda E, hf=hf: E.matmul(scp[:, hf * 512:(hf + 1) * 512], keys_sb[:], qr_sb[:, hf * 512:(hf + 1) * 512], start=True, stop=True),
                   inc="PE" if hf == 1 else None)
        pesc = P.c("PE")
        P.wait("vector", "PE", pesc)
        vsc2 = P.emit("vector", lambda E: E.tensor_copy(scT[:], scp[:]), inc="VE")
        tp_free = 0
        gtp_free = 0
        gg_prev = 0
        for qt in range(8):
            P.wait("tensor", "VE", vsc2)
            P.wait("tensor", "VE", tp_free)
            ptp = P.emit("tensor", lambda E, qt=qt: E.transpose(tpp[:], scT[:, qt * 128:(qt + 1) * 128], ident[0:64, 0:64]), inc="PE")
            P.wait("vector", "PE", ptp)
            tp_free = P.emit("vector", lambda E, qt=qt: E.tensor_copy(stok[:, qt * 64:(qt + 1) * 64], tpp[:]), inc="VE")
            P.emit("vector", lambda E, qt=qt: E.max(out=r16[:, 0:8], in_=stok[:, qt * 64:(qt + 1) * 64]))
            vflush()
            P.emit("vector", lambda E, qt=qt: E.tensor_scalar(mz[:], stok[:, qt * 64:(qt + 1) * 64], r16[:, 7:8], None, ALU.is_ge))
            P.emit("vector", lambda E, qt=qt: E.scalar_tensor_tensor(sa2[:], mz[:], NEG, stok[:, qt * 64:(qt + 1) * 64], ALU.mult, ALU.add))
            P.emit("vector", lambda E: E.max(out=r16[:, 8:16], in_=sa2[:]))
            vflush()
            P.emit("vector", lambda E: E.tensor_scalar(nrm[:], r16[:, 0:1], -1.0, None, ALU.mult))
            vr = P.emit("vector", lambda E, qt=qt: E.tensor_scalar(m01[:], stok[:, qt * 64:(qt + 1) * 64], r16[:, 15:16], None, ALU.is_ge), inc="VE")
            P.wait("scalar", "VE", vr)
            P.emit("scalar", lambda E: E.activation(e16[:], r16[:], AF.Exp, bias=nrm[:]))
            ae = P.emit("scalar", lambda E, qt=qt: E.activation(expv[:], stok[:, qt * 64:(qt + 1) * 64], AF.Exp, bias=nrm[:]), inc="AC")
            P.wait("vector", "AC", ae)
            P.emit("vector", lambda E: E.reduce_sum(den6[:], e16[:], AX.X))
            vflush()
            P.emit("vector", lambda E: E.reciprocal(rden6[:], den6[:]))
            vflush()
            P.wait("vector", "PE", gg_prev)
            vg = P.emit("vector", lambda E: E.scalar_tensor_tensor(gtok[:], expv[:], rden6[:], m01[:], ALU.mult, ALU.mult), inc="VE")
            P.wait("tensor", "VE", vg)
            gg = P.emit("tensor", lambda E: E.transpose(gtp[:], gtok[:], ident[:]), inc="PE")
            gg_prev = gg
            P.wait("vector", "PE", gg)
            gtp_free = P.emit("vector", lambda E, qt=qt: E.tensor_copy(gateT[:, qt * 128:(qt + 1) * 128], gtp[:]), inc="VE")
        P.barrier()

        awslab = V(ACT2, 0, G * 512)
        zg = V(BFS, 0, 4 * T)
        grb = V(ACT3, 2 * T, T)
        afb = [V(ACT3, 3 * T, 512), V(ACT3, 3 * T + 512, 512)]
        rrep_sb = V(ACT3, 4 * T, 512, parts=64)
        zp = [V(PA, 0, T), V(PB, 0, T)]
        grp = V(PC, 0, T)
        cr = P.emit("sync", lambda E: E.dma_start(out=rrep_sb[:].rearrange("p (m n) -> p m n", n=128),
                                                  in_=rrep_i.rearrange("m p n -> p m n")), inc="LD", n=16)
        afree = {}
        for g in range(G):
            i = g % 2
            P.wait("sync", "VE", afree.get(i, 0))
            c1 = P.emit("sync", lambda E, g=g, b=afb[i]: E.dma_start(out=b[:], in_=aw_t[g]), inc="LD", n=16)
            P.wait("vector", "LD", c1)
            afree[i] = P.emit("vector", lambda E, g=g, b=afb[i]: E.tensor_copy(awslab[:, g * 512:(g + 1) * 512], b[:]), inc="VE")
        ve_aw = P.c("VE")
        zp_free = {}
        grb_free = 0
        for m in range(4):
            mb = m % 2
            P.wait("tensor", "LD", cr)
            P.wait("tensor", "VE", grb_free)
            for hf in range(2):
                P.emit("tensor", lambda E, m=m, hf=hf:
                       E.matmul(grp[:, hf * 512:(hf + 1) * 512], rrep_sb[:, m * 128:(m + 1) * 128],
                                gateT[:, hf * 512:(hf + 1) * 512], start=True, stop=True),
                       inc="PE" if hf == 1 else None)
            peg = P.c("PE")
            P.wait("vector", "PE", peg)
            veg = P.emit("vector", lambda E: E.tensor_copy(grb[:], grp[:]), inc="VE")
            grb_free = veg
            P.wait("tensor", "VE", ve_aw)
            P.wait("tensor", "VE", zp_free.get(mb, 0))
            for g in range(G):
                for hf in range(2):
                    P.emit("tensor", lambda E, g=g, m=m, hf=hf, mb=mb:
                           E.matmul(zp[mb][:, hf * 512:(hf + 1) * 512],
                                    awslab[:, g * 512 + m * 128: g * 512 + (m + 1) * 128],
                                    xn2[:, g * T + hf * 512: g * T + (hf + 1) * 512],
                                    start=(g == 0), stop=(g == G - 1)),
                           inc="PE" if (g == G - 1 and hf == 1) else None)
            pez = P.c("PE")
            P.wait("vector", "PE", pez)
            P.wait("vector", "VE", veg)
            zp_free[mb] = P.emit("vector", lambda E, m=m, mb=mb:
                                 E.scalar_tensor_tensor(zg[:, m * T:(m + 1) * T], zp[mb][:], 1.0, grb[:], ALU.bypass, ALU.mult), inc="VE")
        ve_zg = P.c("VE")
        if debug:
            P.wait("gpsimd", "VE", P.c("VE"))
            for nm, src in [("d_dens", dens32), ("d_negm", negmb), ("d_gate", gateT),
                            ("d_zg", zg), ("d_xn2", xn2)]:
                if nm in dbg:
                    P.emit("gpsimd", lambda E, d=dbg[nm], s=src: E.dma_start(out=d, in_=s[:]), inc="ST", n=16)
        P.barrier()

        bw = V(KVW, 0, 4 * 512)
        bfb = [V(ACT3, 2 * T, 512), V(ACT3, 2 * T + 512, 512)]
        xb7 = [V(ACT3, 3 * T, T), V(ACT3, 4 * T, T)]
        ost = [V(ACT3, 5 * T, T), V(ACT3, 6 * T, T)]
        bp = [V(PA, 0, T), V(PB, 0, T)]
        bfree = {}
        bw_free = 0
        bp_free = {}
        ost_free = {}
        for s in range(8):
            for g in range(4):
                i = g % 2
                P.wait("sync", "VE", bfree.get(i, 0))
                c1 = P.emit("sync", lambda E, s=s, g=g, b=bfb[i]: E.dma_start(out=b[:], in_=bw_t[s, g]), inc="LD", n=16)
                P.wait("vector", "LD", c1)
                if g == 0:
                    P.wait("vector", "PE", bw_free)
                bfree[i] = P.emit("vector", lambda E, g=g, b=bfb[i]:
                                  E.tensor_scalar(bw[:, g * 512:(g + 1) * 512], b[:], 2.0, None, ALU.mult), inc="VE")
            ve_bw = P.c("VE")
            for m in range(4):
                mt = s * 4 + m
                mb = mt % 2
                P.wait("sync", "VE", ost_free.get(("x", mb), 0))
                cx = P.emit("sync", lambda E, mt=mt, mb=mb: E.dma_start(out=xb7[mb][:], in_=x1_scr[mt]), inc="LD", n=16)
                P.wait("tensor", "VE", ve_bw)
                P.wait("tensor", "VE", ve_zg)
                P.wait("tensor", "VE", bp_free.get(mb, 0))
                for g in range(4):
                    for hf in range(2):
                        P.emit("tensor", lambda E, g=g, m=m, hf=hf, mb=mb:
                               E.matmul(bp[mb][:, hf * 512:(hf + 1) * 512],
                                        bw[:, g * 512 + m * 128: g * 512 + (m + 1) * 128],
                                        zg[:, g * T + hf * 512: g * T + (hf + 1) * 512],
                                        start=(g == 0), stop=(g == 3)),
                               inc="PE" if (g == 3 and hf == 1) else None)
                pem = P.c("PE")
                P.wait("vector", "PE", pem)
                P.wait("vector", "LD", cx)
                P.wait("vector", "ST", ost_free.get(mb, 0))
                bp_free[mb] = P.emit("vector", lambda E, mb=mb: E.tensor_add(ost[mb][:], bp[mb][:], xb7[mb][:]), inc="VE")
                ost_free[("x", mb)] = bp_free[mb]
                stq = "gpsimd" if mt % 2 == 0 else "scalar"
                P.wait(stq, "VE", bp_free[mb])
                ost_free[mb] = P.emit(stq, lambda E, mt=mt, mb=mb: E.dma_start(out=out_t[mt], in_=ost[mb][:]), inc="ST", n=16)
            bw_free = P.c("PE")

        P.wait("gpsimd", "ST", P.c("ST"))

        block = stack.enter_context(nc.Block())

        def mk(engname):
            def f(eng):
                for fn in P.q[engname]:
                    fn(eng)
            return f
        block.sync(mk("sync"))
        block.gpsimd(mk("gpsimd"))
        block.vector(mk("vector"))
        block.scalar(mk("scalar"))
        block.tensor(mk("tensor"))

    return nc


def check_deadlock(P):
    pos = {e: 0 for e in ENGINES}
    cnt = {sname: 0 for sname in SEMS}
    total = sum(len(P.trace[e]) for e in ENGINES)
    done = 0
    progress = True
    while progress:
        progress = False
        for e in ENGINES:
            tr = P.trace[e]
            while pos[e] < len(tr):
                it = tr[pos[e]]
                if it[0] == "wait" and cnt[it[1]] < it[2]:
                    break
                if it[0] == "inc":
                    cnt[it[1]] += it[2]
                pos[e] += 1
                done += 1
                progress = True
    stuck = {e: (pos[e], len(P.trace[e]), P.trace[e][pos[e]] if pos[e] < len(P.trace[e]) else None) for e in ENGINES}
    return done == total, stuck, cnt


def _prep_codes(codes, nslab):
    return np.ascontiguousarray(
        codes.reshape(32, 128, nslab, 256).transpose(2, 1, 0, 3)).astype(np.int8)


def _prep_inputs(L0_attn_x, q_codes, k_codes, v_codes, o_codes,
                 q_scales, k_scales, v_scales, o_scales,
                 A_w, B_w, Wq_w, keys):
    f32 = np.float32
    shared = {}
    shared["qcod"] = _prep_codes(q_codes, 16)
    shared["ocod"] = _prep_codes(o_codes, 16)
    shared["kcod"] = _prep_codes(k_codes, 4)
    shared["vcod"] = _prep_codes(v_codes, 4)
    shared["qsc"] = np.ascontiguousarray(q_scales, f32)
    shared["osc"] = np.ascontiguousarray(o_scales, f32)
    shared["ksc"] = np.ascontiguousarray(k_scales, f32)
    shared["vsc"] = np.ascontiguousarray(v_scales, f32)
    shared["aw_t"] = np.ascontiguousarray(A_w.T.reshape(32, 128, 512), f32)
    shared["bw_t"] = np.ascontiguousarray(
        B_w.T.reshape(512, 8, 512).transpose(1, 0, 2).reshape(8, 4, 128, 512), f32)
    shared["wqw_t"] = np.ascontiguousarray(Wq_w.T.reshape(32, 128, 16), f32)
    shared["keys_t"] = np.ascontiguousarray(keys.T, f32)

    jm = np.zeros((128, 128), f32)
    for d in range(64):
        jm[d + 64, d] = 1.0
        jm[d, d + 64] = 1.0
    shared["jmat_i"] = jm
    shared["ident_i"] = np.eye(128, dtype=f32)
    rr = np.zeros((4, 64, 128), f32)
    for m in range(4):
        for nn in range(128):
            rr[m, (m * 128 + nn) // 8, nn] = 1.0
    shared["rrep_i"] = rr
    consts = np.zeros((128, 8), f32)
    consts[:, 0] = 1.0
    consts[:, 1] = -1.0
    consts[:, 2] = 1.0 / 4096.0
    shared["consts_i"] = consts

    def rope_tables(pos, scale):
        half = 64
        inv = 1.0 / (10000.0 ** (np.arange(half, dtype=np.float64) / half))
        ang = pos.astype(np.float64)[None, :] * inv[:, None]
        cc = np.concatenate([np.cos(ang), np.cos(ang)], axis=0) * scale
        ss = np.concatenate([-np.sin(ang), np.sin(ang)], axis=0) * scale
        return np.ascontiguousarray(cc, f32), np.ascontiguousarray(ss, f32)

    in_maps = []
    sd = 1.0 / np.sqrt(128.0)
    for c in range(8):
        b, c4 = c // 4, c % 4
        d = dict(shared)
        d["xq_t"] = np.ascontiguousarray(L0_attn_x[b, c4 * 1024:(c4 + 1) * 1024, :].T, f32)
        qpos = np.arange(c4 * 1024, (c4 + 1) * 1024)
        d["ropeq_cc"], d["ropeq_ss"] = rope_tables(qpos, sd)
        d["ropeko_cc"], d["ropeko_ss"] = rope_tables(qpos, 1.0)
        d["kqpos"] = np.ascontiguousarray(qpos[None, :], f32)
        kp = np.arange(32)[None, :] * 128 + np.arange(128)[:, None]
        d["kpcol"] = np.ascontiguousarray(kp, f32)
        bias = np.zeros((128, 32), f32)
        bias[:, (c4 + 1) * 8:] = NEG
        d["biascol_i"] = bias
        in_maps.append(d)
    return in_maps


def kernel(L0_attn_x, L0_attn_q_codes, L0_attn_k_codes, L0_attn_v_codes, L0_attn_o_codes,
           L0_attn_q_scales, L0_attn_k_scales, L0_attn_v_scales, L0_attn_o_scales,
           L0_ffn_A_w, L0_ffn_B_w, L0_ffn_Wq_w, L0_ffn_keys):
    in_maps = _prep_inputs(np.asarray(L0_attn_x), np.asarray(L0_attn_q_codes),
                           np.asarray(L0_attn_k_codes), np.asarray(L0_attn_v_codes),
                           np.asarray(L0_attn_o_codes), np.asarray(L0_attn_q_scales),
                           np.asarray(L0_attn_k_scales), np.asarray(L0_attn_v_scales),
                           np.asarray(L0_attn_o_scales), np.asarray(L0_ffn_A_w),
                           np.asarray(L0_ffn_B_w), np.asarray(L0_ffn_Wq_w),
                           np.asarray(L0_ffn_keys))
    nc = build_program(debug=False)
    res = run_bass_kernel_spmd(nc, in_maps, core_ids=list(range(8)))
    out = np.empty((2, 4096, 4096), np.float32)
    for c in range(8):
        b, c4 = c // 4, c % 4
        o = res.results[c]["out_t"].reshape(4096, 1024)
        out[b, c4 * 1024:(c4 + 1) * 1024, :] = o.T
    return out
